# version 18
# baseline (speedup 1.0000x reference)
import numpy as np
import ml_dtypes
import concourse.bass as bass
import concourse.mybir as mybir
from concourse.bass_utils import run_bass_kernel_spmd
from concourse.bass_isa import ReduceOp
from concourse.library_config import attn as attn_lib
from concourse.library_overlay import lower_extended_insts

F32 = mybir.dt.float32
BF16 = mybir.dt.bfloat16
I32 = mybir.dt.int32
AF = mybir.ActivationFunctionType
OP = mybir.AluOpType
AX = mybir.AxisListType

D = 1024
S = 4096
QL = 2048
NB = 8
NG = 8
NQC = 4
NKC = 32
NKG = 16
MAGIC_EPS = 1e-5

ENGINES = ("sync", "vector", "scalar", "tensor", "gpsimd")


class Sems:

    NAMES = ("x", "xg", "xc", "xq", "fin", "sd", "rs", "bc", "xn", "wd", "wg", "wc",
             "ps", "ev", "q16", "mxv", "mxg", "acst", "xrel",
             "qk", "exp", "pvp", "rcp", "rcb", "a16", "mlt", "asp",
             "arel", "accr", "out")

    def __init__(self, nc):
        self.dummies = [nc.alloc_semaphore(f"s_dummy{i}") for i in range(24)]
        self.h = {n: nc.alloc_semaphore("s_" + n) for n in self.NAMES}
        self.reset()

    def reset(self):
        self.n = {k: 0 for k in self.NAMES}


class Em:

    def __init__(self, engine_name, eng, sems):
        self.who = engine_name
        self.eng = eng
        self.s = sems

    def op(self, who, fn, sem=None, k=1):
        if who == self.who:
            ins = fn(self.eng)
            if sem is not None:
                ins.then_inc(self.s.h[sem], k)
        if sem is not None:
            self.s.n[sem] += k

    def wait(self, who, sem, val=None):
        if who == self.who:
            v = self.s.n[sem] if val is None else val
            if v > 0:
                self.eng.wait_ge(self.s.h[sem], v)

    def cnt(self, sem):
        return self.s.n[sem]


def build_core_program(debug=False):
    nc = bass.Bass("TRN2", target_bir_lowering=False, detect_race_conditions=False)
    dbg_kind = "ExternalOutput" if debug else "Internal"

    xT = nc.dram_tensor("xT", [D, S], F32, kind="ExternalInput")
    wq = nc.dram_tensor("wq", [D, D], F32, kind="ExternalInput")
    wk = nc.dram_tensor("wk", [D, D], F32, kind="ExternalInput")
    wv = nc.dram_tensor("wv", [D, D], F32, kind="ExternalInput")
    wo = nc.dram_tensor("wo", [D, D], F32, kind="ExternalInput")
    boc = nc.dram_tensor("boc", [128, 8], F32, kind="ExternalInput")
    wfT = nc.dram_tensor("wfT", [D, D], F32, kind="ExternalInput")
    h16d = nc.dram_tensor("h16d", [D, D], BF16, kind="ExternalInput")
    outT = nc.dram_tensor("outT", [D, QL], F32, kind="ExternalOutput")

    acc_sp = nc.dram_tensor("acc_sp", [D, QL], F32, kind=dbg_kind)
    a_sp = nc.dram_tensor("a_sp", [D, QL], BF16, kind=dbg_kind)
    xn_dbg = nc.dram_tensor("xn_dbg", [D, S], BF16, kind=dbg_kind) if debug else None
    if debug:
        kT_dbg = nc.dram_tensor("kT_dbg", [128, S], BF16, kind=dbg_kind)
        qT_dbg = nc.dram_tensor("qT_dbg", [128, QL], BF16, kind=dbg_kind)
        vg_dbg = nc.dram_tensor("vg_dbg", [128, NKC * 130], BF16, kind=dbg_kind)
        rr_dbg = nc.dram_tensor("rr_dbg", [128, 512], F32, kind=dbg_kind)
        p16_dbg = nc.dram_tensor("p16_dbg", [128, 1024], BF16, kind=dbg_kind)
        xh_dbg = nc.dram_tensor("xh_dbg", [D, 512], F32, kind=dbg_kind)
        hb_dbg = nc.dram_tensor("hb_dbg", [128, 2048], BF16, kind=dbg_kind)
        mx_dbg = nc.dram_tensor("mx_dbg", [128, 2048], F32, kind=dbg_kind)
        q_dbg = nc.dram_tensor("q_dbg", [D, 512], BF16, kind=dbg_kind)

    xn16 = [nc.alloc_sbuf_tensor(f"xn16_{c}", [128, S], BF16) for c in range(8)]
    kT2 = [nc.alloc_sbuf_tensor(f"kT{i}", [128, S], BF16) for i in range(2)]
    vg = nc.alloc_sbuf_tensor("vg", [128, NKC * 130], BF16)
    qT = nc.alloc_sbuf_tensor("qT", [128, QL], BF16)
    p16 = [[nc.alloc_sbuf_tensor(f"p16_{h}_{b}", [128, 1024], BF16)
            for b in range(2)] for h in range(2)]
    asta1 = [nc.alloc_sbuf_tensor(f"asta_{h}", [64, 512], BF16) for h in range(2)]
    rrowT = nc.alloc_sbuf_tensor("rrowT", [128, 512], F32)
    rrow0 = nc.alloc_sbuf_tensor("rrow0", [1, 512], F32)
    rcb = [nc.alloc_sbuf_tensor(f"rcb{i}", [64, 512], F32) for i in range(2)]
    wst = [nc.alloc_sbuf_tensor(f"wst{i}", [128, 1024], F32) for i in range(2)]
    w16 = {m: nc.alloc_sbuf_tensor(f"w16{m}", [128, 1024], BF16) for m in "qkv"}
    ones16 = nc.alloc_sbuf_tensor("ones16", [128, 128], BF16)
    bocs = nc.alloc_sbuf_tensor("bocs", [128, 8], F32)
    xstage = [nc.alloc_sbuf_tensor(f"xstage{c}", [128, 512], F32) for c in range(8)]

    REGION = 66 * 1024
    arena = nc.alloc_sbuf_tensor("arena", [128, REGION], mybir.dt.uint8)
    arena_addr = nc.lookup_mloc(arena).addr

    def overlay(name, shape, dtype, off):
        per_part = int(np.prod(shape[1:])) * mybir.dt.size(dtype)
        assert off + per_part <= REGION, (name, off, per_part)
        h = nc.virtual_sbuf_tensor(name, shape, dtype)
        nc._tensor(h.name, list(shape), dtype, type="SB")
        ml = nc.lookup_mloc(h)
        ml.allocated = not nc.target_bir_lowering
        ml.addr = arena_addr + off
        return h

    o = 0
    tern16 = []
    for c in range(8):
        tern16.append(overlay(f"tern16_{c}", [128, D], BF16, o)); o += 2048
    xh32 = []
    for j in range(8):
        xh32.append(overlay(f"xh32_{j}", [128, 512], F32, o)); o += 2048
    q16t = []
    for j in range(8):
        q16t.append(overlay(f"q16t_{j}", [128, 512], BF16, 16384 + j * 2048))
    hbufB = []
    for p in range(8):
        hbufB.append(overlay(f"hbufB{p}", [128, 1024], BF16, o)); o += 2048
    O_MX = o
    mx1 = overlay("mx1", [128, 512], F32, o); o += 2048
    mx2 = overlay("mx2", [128, 512], F32, o); o += 2048
    r7B = overlay("r7B", [128, 512], F32, o); o += 2048
    srB = overlay("srB", [128, 512], F32, o); o += 2048
    wmB = overlay("wmB", [128, D], F32, o)
    fi32 = overlay("fi32", [128, 512], I32, o); o += 2048
    ff1 = overlay("ff1", [128, 512], F32, o); o += 2048
    ff2 = overlay("ff2", [128, 512], F32, o); o += 2048
    r_wm = overlay("r_wm", [1, D], F32, 16384 + 4 * 2048)
    r_rw = overlay("r_rw", [1, D], F32, 16384 + 6 * 2048)
    rwB = overlay("rwB", [128, D], F32, O_MX)
    F_END = o
    assert F_END <= REGION

    o = 0
    xb16 = []
    for c in range(8):
        xb16.append(overlay(f"xb16_{c}", [128, 512], BF16, o)); o += 1024
    xsq16 = []
    for c in range(8):
        xsq16.append(overlay(f"xsq16_{c}", [128, 512], BF16, o)); o += 1024
    xodd = []
    for c in range(8):
        xodd.append(overlay(f"xodd_{c}", [128, 512], F32, o)); o += 2048
    r_mean = [overlay("r_mean0", [1, 512], F32, o), overlay("r_mean1", [1, 512], F32, o + 2048)]
    o += 4096
    r_ms = [overlay("r_ms0", [1, 512], F32, o), overlay("r_ms1", [1, 512], F32, o + 2048)]
    o += 4096
    r_c = [overlay("r_c0", [1, 512], F32, o), overlay("r_c1", [1, 512], F32, o + 2048)]
    o += 4096
    meanB = [overlay("meanB0", [128, 512], F32, o), overlay("meanB1", [128, 512], F32, o + 2048)]
    o += 4096
    rsB = [overlay("rsB0", [128, 512], F32, o), overlay("rsB1", [128, 512], F32, o + 2048)]
    o += 4096
    tA = [overlay("tA", [128, 512], F32, o)] * 2
    o += 2048
    assert o <= REGION

    o = 0
    wo16 = []
    for c in range(8):
        wo16.append(overlay(f"wo16_{c}", [128, D], BF16, o)); o += 2048
    arel = []
    for c in range(8):
        arel.append(overlay(f"arel{c}", [128, 512], BF16, o)); o += 1024
    outst = [overlay("outst0", [128, 512], F32, o), overlay("outst1", [128, 512], F32, o + 2048)]
    o += 4096
    D_END = o
    p16x = [overlay(f"p16x{h}", [128, 1024], BF16, D_END + h * 2048) for h in range(2)]
    av65 = [overlay(f"av65_{h}", [65, 512], F32, D_END + 4096 + h * 2048) for h in range(2)]
    qT2b = overlay("qT2b", [128, 2048], BF16, D_END + 8192)
    assert o <= REGION

    fost = [mx1, mx2]

    pbS = [nc.alloc_psum_tensor(f"pbS{i}", [128, 1024], F32) for i in range(2)]
    pbA2 = [[nc.alloc_psum_tensor(f"pbA{h}_{p}", [128, 512], F32) for p in range(2)]
            for h in range(2)]
    pbT = [pbS[0][:, 0:512], pbS[1][:, 0:512]]

    for h in range(2):
        p16[h].append(p16x[h])
    qT2 = [qT, qT2b]
    sems = Sems(nc)

    def walk(em: Em):
        sems.reset()
        st = {"tile": 0, "ev_snap": [],
              "sbank_free": [0, 0],
              "pbuf_free": [[0, 0, 0], [0, 0, 0]],
              "pvpsum_free": [0, 0],
              "w16_free": 0,
              "rcb_free": [0, 0],
              "av_free": [0, 0],
              "rrow_free": [0, 0],
              "asta_free": 0,
              "outst_free": [0, 0],
              }

        def acc_tile(n_mm, mk_mm, evac, banks=None, te_pre=None):
            k = st["tile"]
            st["tile"] += 1
            bank = (banks or pbT)[k % 2]
            if k >= 2:
                em.wait("tensor", "ev", st["ev_snap"][k - 2] + 1)
            if te_pre is not None:
                te_pre()
            for i in range(n_mm):
                em.op("tensor", lambda t, i=i, bank=bank: mk_mm(t, i, bank),
                      sem="ps" if i == n_mm - 1 else None)
            em.wait("vector", "ps", em.cnt("ps"))
            st["ev_snap"].append(em.cnt("ev"))
            evac(bank)

        em.op("gpsimd", lambda g: g.load_library(attn_lib))

        em.op("vector", lambda v: v.memset(ones16[:], 1.0))
        for pb in (pbS[0], pbS[1], pbA2[0][0], pbA2[0][1], pbA2[1][0], pbA2[1][1]):
            em.op("vector", lambda v, pb=pb: v.memset(pb[:], 0.0))
        if debug:
            em.op("vector", lambda v: v.memset(hbufB[0][:], 777.0))
            em.op("vector", lambda v: v.memset(hbufB[1][:], 777.0))
        vgr = vg[:].rearrange("p (t s c) -> p t s c", t=NKC, s=2, c=65)
        em.op("vector", lambda v: v.memset(vgr[:, :, :, 64:65], 1.0))
        em.op("sync", lambda sy: sy.dma_start(out=bocs[:], in_=boc[:]), sem="wd", k=16)

        for tb in range(NB):
            par = tb % 2
            xbl = xstage if par == 0 else xodd
            if tb >= 2:
                em.wait("sync", "xn", tb - 1)
                em.wait("gpsimd", "xn", tb - 1)
            em.wait("sync", "x", em.cnt("x"))
            em.wait("gpsimd", "xg", em.cnt("xg"))
            for c in range(8):
                who = "sync" if c < 4 else "gpsimd"
                em.op(who, lambda e, c=c: e.dma_start(
                    out=xbl[c][:], in_=xT[c * 128:(c + 1) * 128, tb * 512:(tb + 1) * 512]),
                    sem="x" if c < 4 else "xg", k=16)
            em.wait("vector", "x", em.cnt("x"))
            em.wait("vector", "xg", em.cnt("xg"))
            em.wait("scalar", "x", em.cnt("x"))
            em.wait("scalar", "xg", em.cnt("xg"))
            for c in range(8):
                em.op("vector", lambda v, c=c: v.tensor_copy(out=xb16[c][:], in_=xbl[c][:]),
                      sem="xc" if c == 7 else None)
                em.op("scalar", lambda sc, c=c: sc.activation(
                    out=xsq16[c][:], in_=xbl[c][:], func=AF.Square),
                    sem="xq" if c == 7 else None)

            def mk_sum(t, i, bank, src):
                return t.matmul(bank[:], ones16[:], src[i][:],
                                start=(i == 0), stop=(i == 7))

            def ev_sum(bank, dst):
                em.op("vector", lambda v: v.tensor_scalar(
                    out=dst[:], in0=bank[0:1, :],
                    scalar1=1.0 / 1024.0, scalar2=None, op0=OP.mult), sem="ev")

            em.wait("tensor", "xc", em.cnt("xc"))
            em.wait("tensor", "xq", em.cnt("xq"))
            acc_tile(8, lambda t, i, bank: mk_sum(t, i, bank, xb16),
                     lambda bank: ev_sum(bank, r_mean[par]))
            acc_tile(8, lambda t, i, bank: mk_sum(t, i, bank, xsq16),
                     lambda bank: ev_sum(bank, r_ms[par]))
            em.wait("vector", "ev", em.cnt("ev"))
            em.op("vector", lambda v: v.tensor_mul(
                r_c[par][:], r_mean[par][:], r_mean[par][:]))
            em.op("vector", lambda v: v.tensor_sub(
                r_ms[par][:], r_ms[par][:], r_c[par][:]))
            em.op("vector", lambda v: v.tensor_scalar_add(
                r_ms[par][:], r_ms[par][:], MAGIC_EPS), sem="fin")
            em.wait("scalar", "fin", em.cnt("fin"))
            em.op("scalar", lambda sc: sc.activation(
                out=r_c[par][:], in_=r_ms[par][:], func=AF.Sqrt), sem="sd")
            em.wait("vector", "sd", em.cnt("sd"))
            em.op("vector", lambda v: v.reciprocal(
                out=r_c[par][:], in_=r_c[par][:]), sem="rs")
            em.wait("gpsimd", "rs", em.cnt("rs"))
            if tb >= 2:
                em.wait("gpsimd", "xn", tb - 1)
            em.op("gpsimd", lambda g: g.partition_broadcast(
                meanB[par][:], r_mean[par][:], 128))
            em.op("gpsimd", lambda g: g.partition_broadcast(
                rsB[par][:], r_c[par][:], 128), sem="bc")
            em.wait("vector", "bc", em.cnt("bc"))
            for c in range(8):
                em.op("vector", lambda v, c=c: v.tensor_sub(tA[par][:], xbl[c][:], meanB[par][:]))
                em.op("vector", lambda v, c=c: v.tensor_mul(
                    xn16[c][:, tb * 512:(tb + 1) * 512], tA[par][:], rsB[par][:]),
                    sem="xn" if c == 7 else None)

        if debug:
            em.wait("gpsimd", "xn", NB)
            for c in range(8):
                em.op("gpsimd", lambda gp, c=c: gp.dma_start(
                    out=xn_dbg[c * 128:(c + 1) * 128, :], in_=xn16[c][:]), sem="out", k=16)

        for c in range(8):
            em.wait("sync", "wc", em.cnt("wc") - 1)
            em.op("sync", lambda sy, c=c: sy.dma_start(
                out=wst[c % 2][:], in_=wfT[c * 128:(c + 1) * 128, :]), sem="wd", k=16)
            em.wait("scalar", "wd", em.cnt("wd"))
            em.op("scalar", lambda sc, c=c: sc.activation(
                out=tern16[c][:], in_=wst[c % 2][:], func=AF.Abs), sem="wc")

        def mk_wsum(t, i, bank, half):
            return t.matmul(bank[:], ones16[:],
                            tern16[i][:, half * 512:(half + 1) * 512],
                            start=(i == 0), stop=(i == 7))

        for half in range(2):
            def ev_wsum(bank, half=half):
                em.op("vector", lambda v: v.tensor_scalar(
                    out=r_wm[0:1, half * 512:(half + 1) * 512], in0=bank[0:1, :],
                    scalar1=1.0 / 1024.0, scalar2=None, op0=OP.mult), sem="ev")
            em.wait("tensor", "wc", em.cnt("wc"))
            acc_tile(8, lambda t, i, bank, half=half: mk_wsum(t, i, bank, half), ev_wsum)
        em.op("vector", lambda v: v.reciprocal(out=r_rw[:], in_=r_wm[:]), sem="rs")
        em.wait("gpsimd", "rs", em.cnt("rs"))
        em.op("gpsimd", lambda g: g.partition_broadcast(rwB[:], r_rw[:], 128))
        em.op("gpsimd", lambda g: g.partition_broadcast(wmB[:], r_wm[:], 128), sem="bc")

        for c in range(8):
            em.wait("sync", "wc", em.cnt("wc") - 1)
            em.op("sync", lambda sy, c=c: sy.dma_start(
                out=wst[c % 2][:], in_=wfT[c * 128:(c + 1) * 128, :]), sem="wd", k=16)
            em.wait("vector", "wd", em.cnt("wd"))
            em.wait("vector", "bc", em.cnt("bc"))
            em.op("vector", lambda v, c=c: v.tensor_mul(xh32[0][:], wst[c % 2][:, 0:512], rwB[:, 0:512]))
            em.op("vector", lambda v, c=c: v.tensor_mul(xh32[1][:], wst[c % 2][:, 512:1024], rwB[:, 512:1024]))
            for half in range(2):
                em.op("vector", lambda v, c=c, half=half: v.tensor_scalar(
                    out=xh32[2][:], in0=xh32[half][:], scalar1=0.5, scalar2=None, op0=OP.is_gt))
                em.op("vector", lambda v, c=c, half=half: v.scalar_tensor_tensor(
                    out=xh32[2][:], in0=xh32[half][:], scalar=-0.5, in1=xh32[2][:],
                    op0=OP.is_ge, op1=OP.add))
                em.op("vector", lambda v, c=c, half=half: v.tensor_scalar(
                    out=xh32[2][:], in0=xh32[2][:],
                    scalar1=1.0, scalar2=None, op0=OP.subtract))
                em.op("vector", lambda v, c=c, half=half: v.tensor_mul(
                    tern16[c][:, half * 512:(half + 1) * 512], xh32[2][:],
                    wmB[:, half * 512:(half + 1) * 512]),
                    sem="wc" if half == 1 else None)

        def load_w_group(g):
            for mi, (mat, wdr) in enumerate((("q", wq), ("k", wk), ("v", wv))):
                em.wait("sync", "wc", em.cnt("wc"))
                em.op("sync", lambda sy, wdr=wdr, g=g, mi=mi: sy.dma_start(
                    out=wst[mi % 2][:], in_=wdr[g * 128:(g + 1) * 128, :]),
                    sem="wd", k=16)
                em.wait("vector", "wd", em.cnt("wd"))
                em.wait("vector", "ps", st["w16_free"])
                em.op("vector", lambda v, mat=mat, mi=mi: v.tensor_copy(
                    out=w16[mat][:], in_=wst[mi % 2][:]), sem="wc")

        def emit_k_tiles(g, banks, te_pre_first):
            for t8 in range(8):
                def mk_k(t, i, bank, t8=t8, g=g):
                    return t.matmul(bank[:], w16["k"][:, i * 128:(i + 1) * 128],
                                    xn16[i][:, t8 * 512:(t8 + 1) * 512],
                                    start=(i == 0), stop=(i == 7))

                def ev_k(bank, t8=t8, g=g):
                    em.op("vector", lambda v: v.tensor_copy(
                        out=kT2[g % 2][:, t8 * 512:(t8 + 1) * 512], in_=bank[:]), sem="ev")
                acc_tile(8, mk_k, ev_k, banks=banks,
                         te_pre=te_pre_first if t8 == 0 else None)

        load_w_group(0)
        em.wait("tensor", "xn", NB)
        em.wait("tensor", "wc", em.cnt("wc"))
        for tq in range(NQC):
            def mk_q0(t, i, bank, tq=tq):
                return t.matmul(bank[:], w16["q"][:, i * 128:(i + 1) * 128],
                                xn16[i][:, tq * 512:(tq + 1) * 512],
                                start=(i == 0), stop=(i == 7))

            def ev_q0(bank, tq=tq):
                em.op("vector", lambda v: v.tensor_scalar(
                    out=qT2[0][:, tq * 512:(tq + 1) * 512], in0=bank[:],
                    scalar1=0.125, scalar2=None, op0=OP.mult), sem="ev")
            acc_tile(8, mk_q0, ev_q0)
        emit_k_tiles(0, pbT, None)

        em.wait("sync", "xn", NB)
        for j in range(8):
            em.op("sync", lambda sy, j=j: sy.dma_start(
                out=hbufB[j][:], in_=h16d[j * 128:(j + 1) * 128, :]),
                sem="wg", k=16)
        for tc in range(NQC):
            if tc >= 1:
                em.wait("sync", "acst", em.cnt("acst"))
            for c in range(8):
                em.op("sync", lambda sy, c=c, tc=tc: sy.dma_start(
                    out=xstage[c][:], in_=xT[c * 128:(c + 1) * 128, tc * 512:(tc + 1) * 512]),
                    sem="xrel", k=16)
            if tc == 0:
                em.wait("tensor", "wg", em.cnt("wg"))
            for j in range(8):
                def mk_xh(t, i, bank, j=j, tc=tc):
                    return t.matmul(bank[:], hbufB[j][:, i * 128:(i + 1) * 128],
                                    xn16[i][:, tc * 512:(tc + 1) * 512],
                                    start=(i == 0), stop=(i == 7))

                def ev_xh(bank, j=j):
                    em.op("vector", lambda v: v.tensor_copy(out=xh32[j][:], in_=bank[:]))
                    if j == 0:
                        em.wait("vector", "acst", em.cnt("acst"))
                        em.op("vector", lambda v: v.tensor_copy(out=mx1[:], in_=xh32[j][:]))
                        em.op("vector", lambda v: v.tensor_copy(out=mx2[:], in_=xh32[j][:]), sem="ev")
                    else:
                        em.op("vector", lambda v: v.tensor_max(mx1[:], mx1[:], xh32[j][:]))
                        em.op("vector", lambda v: v.tensor_tensor(mx2[:], mx2[:], xh32[j][:], OP.min),
                              sem="ev")
                    if j == 7:
                        em.op("vector", lambda v: v.tensor_scalar_mul(mx2[:], mx2[:], -1.0))
                        em.op("vector", lambda v: v.tensor_max(mx1[:], mx1[:], mx2[:]), sem="mxv")

                acc_tile(8, mk_xh, ev_xh, banks=[pbA2[0][0][:], pbA2[0][1][:]])

            em.wait("gpsimd", "mxv", em.cnt("mxv"))
            if debug and tc == 0:
                em.op("gpsimd", lambda g: g.dma_start(
                    out=mx_dbg[:, 1536:2048], in_=mx1[:]), sem="asp", k=16)
            em.op("gpsimd", lambda g: g.partition_all_reduce(srB[:], mx1[:], 128, ReduceOp.max),
                  sem="mxg")
            em.wait("vector", "mxg", em.cnt("mxg"))
            em.op("vector", lambda v: v.tensor_scalar_max(mx2[:], srB[:], MAGIC_EPS))
            em.op("vector", lambda v: v.reciprocal(out=r7B[:], in_=mx2[:]))
            em.op("vector", lambda v: v.tensor_scalar_mul(r7B[:], r7B[:], 7.0))
            em.op("vector", lambda v: v.tensor_scalar_mul(srB[:], mx2[:], 1.0 / 7.0))
            for j in range(8):
                em.op("vector", lambda v, j=j: v.tensor_mul(ff1[:], xh32[j][:], r7B[:]))
                em.op("vector", lambda v: v.tensor_copy(out=fi32[:], in_=ff1[:]))
                em.op("vector", lambda v: v.tensor_copy(out=ff2[:], in_=fi32[:]))
                em.op("vector", lambda v: v.tensor_sub(ff1[:], ff1[:], ff2[:]))
                em.op("vector", lambda v: v.scalar_tensor_tensor(
                    out=ff2[:], in0=ff1[:], scalar=0.5, in1=ff2[:], op0=OP.is_gt, op1=OP.add))
                em.op("vector", lambda v: v.scalar_tensor_tensor(
                    out=ff2[:], in0=ff1[:], scalar=-0.5, in1=ff2[:], op0=OP.is_ge, op1=OP.add))
                em.op("vector", lambda v: v.tensor_scalar(
                    out=ff2[:], in0=ff2[:], scalar1=1.0, scalar2=7.0, op0=OP.subtract, op1=OP.min))
                em.op("vector", lambda v, j=j: v.tensor_scalar_max(q16t[j][:], ff2[:], -8.0),
                      sem="q16" if j == 7 else None)

            if debug and tc == 0:
                em.wait("gpsimd", "q16", em.cnt("q16"))
                for j in range(8):
                    em.op("gpsimd", lambda gp, j=j: gp.dma_start(
                        out=xh_dbg[j * 128:(j + 1) * 128, :], in_=xh32[j][:]), sem="asp", k=16)
                    em.op("gpsimd", lambda gp, j=j: gp.dma_start(
                        out=q_dbg[j * 128:(j + 1) * 128, :], in_=q16t[j][:]), sem="asp", k=16)
                em.op("gpsimd", lambda gp: gp.dma_start(out=hb_dbg[:, 0:1024], in_=hbufB[0][:]), sem="asp", k=16)
                em.op("gpsimd", lambda gp: gp.dma_start(out=hb_dbg[:, 1024:2048], in_=hbufB[1][:]), sem="asp", k=16)
                em.op("gpsimd", lambda gp: gp.dma_start(out=mx_dbg[:, 0:512], in_=mx1[:]), sem="asp", k=16)
                em.op("gpsimd", lambda gp: gp.dma_start(out=mx_dbg[:, 512:1024], in_=r7B[:]), sem="asp", k=16)
                em.op("gpsimd", lambda gp: gp.dma_start(out=mx_dbg[:, 1024:1536], in_=srB[:]), sem="asp", k=16)
                em.wait("tensor", "asp", em.cnt("asp"))

            for o in range(8):
                def mk_ffn(t, i, bank, o=o):
                    return t.matmul(bank[:], tern16[i][:, o * 128:(o + 1) * 128],
                                    q16t[i][:], start=(i == 0), stop=(i == 7))

                def ev_ffn(bank, o=o, tc=tc):
                    par = o % 2
                    em.wait("vector", "acst", st["outst_free"][par])
                    em.op("vector", lambda v: v.tensor_mul(ff1[:], bank[:], srB[:]))
                    em.wait("vector", "xrel", em.cnt("xrel"))
                    em.op("vector", lambda v: v.tensor_scalar_add(ff2[:], xstage[o][:], bocs[:, o:o + 1]))
                    em.op("vector", lambda v: v.tensor_add(fost[par][:], ff1[:], ff2[:]), sem="ev")
                    em.wait("gpsimd", "ev", em.cnt("ev"))
                    em.op("gpsimd", lambda g: g.dma_start(
                        out=acc_sp[o * 128:(o + 1) * 128, tc * 512:(tc + 1) * 512],
                        in_=fost[par][:]), sem="acst", k=16)
                    st["outst_free"][par] = em.cnt("acst")

                em.wait("tensor", "q16", em.cnt("q16"))
                em.wait("tensor", "wc", em.cnt("wc"))
                acc_tile(8, mk_ffn, ev_ffn, banks=[pbA2[1][0][:], pbA2[1][1][:]])

        for g in range(NG):
            em.wait("tensor", "xn", NB)
            em.wait("tensor", "wc", em.cnt("wc"))
            em.wait("tensor", "exp", em.cnt("exp"))
            def emit_v_tile(tt):
                def mk_v(t, i, bank, tt=tt):
                    return t.matmul(bank[:, 0:128], xn16[i][:, tt * 128:(tt + 1) * 128],
                                    w16["v"][:, i * 128:(i + 1) * 128],
                                    start=(i == 0), stop=(i == 7))

                def ev_v(bank, tt=tt):
                    em.op("vector", lambda v: v.tensor_copy(
                        out=vgr[:, tt, :, 0:64],
                        in_=bank[:, 0:128].rearrange("p (s c) -> p s c", s=2)), sem="ev")

                def pre_v(tt=tt):
                    if tt < 2:
                        em.wait("tensor", "a16", st["pvpsum_free"][1])
                acc_tile(8, mk_v, ev_v, banks=[pbA2[0][1][:], pbA2[1][1][:]], te_pre=pre_v)
                st["v_ready"][tt] = em.cnt("ev")

            st["v_ready"] = [0] * NKC
            st.setdefault("k_ev_snap", 0)

            for qc in range(NQC):
                apar = qc % 2
                exp_done = [[0] * NKG for _ in range(2)]
                for kg in range(NKG + 2):
                    if qc == 0 and kg < NKG:
                        for tt in (2 * kg, 2 * kg + 1):
                            emit_v_tile(tt)
                        if kg == NKG - 1:
                            st["w16_free"] = em.cnt("ps")
                            if g + 1 < NG:
                                load_w_group(g + 1)
                    if qc == 2 and kg % 2 == 0 and kg < NKG and g + 1 < NG:
                        t8 = kg // 2

                        def mk_k2(t, i, bank, t8=t8, gg=g + 1):
                            return t.matmul(bank[:], w16["k"][:, i * 128:(i + 1) * 128],
                                            xn16[i][:, t8 * 512:(t8 + 1) * 512],
                                            start=(i == 0), stop=(i == 7))

                        def ev_k2(bank, t8=t8, gg=g + 1):
                            em.op("vector", lambda v: v.tensor_copy(
                                out=kT2[gg % 2][:, t8 * 512:(t8 + 1) * 512],
                                in_=bank[:]), sem="ev")

                        def pre_k2(t8=t8):
                            if t8 == 0:
                                em.wait("tensor", "a16", st["pvpsum_free"][1])
                        acc_tile(8, mk_k2, ev_k2,
                                 banks=[pbA2[0][1][:], pbA2[1][1][:]], te_pre=pre_k2)
                        st["k_ev_snap"] = em.cnt("ev")
                    if qc == 3 and kg % 2 == 1 and kg < 8 and g + 1 < NG:
                        tq = kg // 2

                        def mk_q2(t, i, bank, tq=tq):
                            return t.matmul(bank[:], w16["q"][:, i * 128:(i + 1) * 128],
                                            xn16[i][:, tq * 512:(tq + 1) * 512],
                                            start=(i == 0), stop=(i == 7))

                        def ev_q2(bank, tq=tq, gg=g + 1):
                            em.op("vector", lambda v: v.tensor_scalar(
                                out=qT2[gg % 2][:, tq * 512:(tq + 1) * 512], in0=bank[:],
                                scalar1=0.125, scalar2=None, op0=OP.mult), sem="ev")

                        def pre_q2(tq=tq):
                            if tq == 0:
                                em.wait("tensor", "a16", st["pvpsum_free"][0])
                                em.wait("tensor", "wc", em.cnt("wc"))
                        acc_tile(8, mk_q2, ev_q2,
                                 banks=[pbA2[0][0][:], pbA2[1][0][:]], te_pre=pre_q2)
                    if kg < NKG:
                        for hh in range(2):
                            em.wait("tensor", "exp", st["sbank_free"][hh])
                            if qc == 0 and kg == 0 and hh == 0:
                                em.wait("tensor", "ev", em.cnt("ev"))
                            for half in range(2):
                                kc = 2 * kg + half
                                em.op("tensor", lambda t, hh=hh, half=half, kc=kc, qc=qc: t.matmul(
                                    pbS[hh][:, half * 512:(half + 1) * 512],
                                    kT2[g % 2][hh * 64:(hh + 1) * 64, kc * 128:(kc + 1) * 128],
                                    qT2[g % 2][hh * 64:(hh + 1) * 64, qc * 512:(qc + 1) * 512],
                                    start=True, stop=True),
                                    sem="qk" if half == 1 else None)
                            em.wait("scalar", "qk", em.cnt("qk"))
                            em.wait("scalar", "pvp", st["pbuf_free"][hh][kg % 3])
                            em.op("scalar", lambda sc, hh=hh, kg=kg: sc.activation(
                                out=p16[hh][kg % 3][:], in_=pbS[hh][:], func=AF.Exp), sem="exp")
                            st["sbank_free"][hh] = em.cnt("exp")
                            exp_done[hh][kg] = em.cnt("exp")
                    if kg >= 2:
                        pk = kg - 2
                        for hh in range(2):
                            em.wait("tensor", "exp", exp_done[hh][pk])
                            if qc == 0 and hh == 0:
                                em.wait("tensor", "ev", st["v_ready"][2 * pk + 1])
                            if pk == 0:
                                em.wait("tensor", "a16", st["pvpsum_free"][apar])
                                if qc == 1 and hh == 0:
                                    em.wait("tensor", "ev", st["v_ready"][NKC - 1])
                                if qc == 3 and hh == 0 and st["k_ev_snap"] > 0:
                                    em.wait("tensor", "ev", st["k_ev_snap"])
                            for half in range(2):
                                kc = 2 * pk + half
                                em.op("tensor", lambda t, hh=hh, half=half, kc=kc, pk=pk, apar=apar: t.matmul(
                                    pbA2[hh][apar][0:65, :],
                                    vgr[:, kc, hh, :],
                                    p16[hh][pk % 3][:, half * 512:(half + 1) * 512],
                                    start=(kc == 0), stop=(kc == NKC - 1)),
                                    sem="pvp" if half == 1 else None)
                            st["pbuf_free"][hh][pk % 3] = em.cnt("pvp")

                em.wait("vector", "pvp", em.cnt("pvp"))
                for hh in range(2):
                    em.wait("vector", "mlt", st["av_free"][hh])
                    em.op("vector", lambda v, hh=hh, apar=apar: v.tensor_copy(
                        out=av65[hh][:], in_=pbA2[hh][apar][0:65, :]), sem="a16")
                st["pvpsum_free"][apar] = em.cnt("a16")
                st["av_free"] = [em.cnt("mlt") + 1, em.cnt("mlt") + 2]
                for hh in range(2):
                    em.wait("vector", "rcb", st["rrow_free"][hh])
                    em.op("vector", lambda v, hh=hh: v.reciprocal(
                        out=av65[hh][64:65, :], in_=av65[hh][64:65, :]))
                    em.op("vector", lambda v, hh=hh: v.tensor_copy(
                        out=rrow0[:], in_=av65[hh][64:65, :]), sem="rcp")
                    em.wait("gpsimd", "rcp", em.cnt("rcp"))
                    em.wait("gpsimd", "mlt", st["rcb_free"][hh])
                    em.op("gpsimd", lambda gp, hh=hh: gp.partition_broadcast(
                        rcb[hh][:], rrow0[:], 64), sem="rcb")
                    st["rrow_free"][hh] = em.cnt("rcb")
                    st["rcb_free"][hh] = em.cnt("mlt") + 1 + hh
                em.wait("vector", "rcb", em.cnt("rcb"))
                em.wait("vector", "asp", st["asta_free"])
                for hh in range(2):
                    em.op("vector", lambda v, hh=hh: v.tensor_mul(
                        asta1[hh][:], av65[hh][0:64, :], rcb[hh][:]), sem="mlt")
                em.wait("gpsimd", "mlt", em.cnt("mlt"))
                for hh in range(2):
                    em.op("gpsimd", lambda gp, g=g, qc=qc, hh=hh: gp.dma_start(
                        out=a_sp[g * 128 + hh * 64:g * 128 + (hh + 1) * 64,
                                 qc * 512:(qc + 1) * 512],
                        in_=asta1[hh][:]), sem="asp", k=16)
                st["asta_free"] = em.cnt("asp")
                if g == NG - 1 and qc == 0:
                    for c in range(8):
                        em.wait("sync", "wc", em.cnt("wc") - (0 if c < 2 else 1))
                        em.op("sync", lambda sy, c=c: sy.dma_start(
                            out=wst[c % 2][:], in_=wo[c * 128:(c + 1) * 128, :]),
                            sem="wd", k=16)
                        em.wait("vector", "wd", em.cnt("wd"))
                        em.op("vector", lambda v, c=c: v.tensor_copy(
                            out=wo16[c][:], in_=wst[c % 2][:]), sem="wc")

        st["outst_free"] = [0, 0]

        for tc in range(NQC):
            em.wait("sync", "asp", em.cnt("asp"))
            if tc >= 1:
                em.wait("sync", "ps", em.cnt("ps"))
                em.wait("sync", "ev", em.cnt("ev"))
            for hd in range(8):
                em.op("sync", lambda sy, hd=hd, tc=tc: sy.dma_start(
                    out=arel[hd][:], in_=a_sp[hd * 128:(hd + 1) * 128, tc * 512:(tc + 1) * 512]),
                    sem="arel", k=16)
            em.wait("sync", "acst", em.cnt("acst"))
            for o in range(8):
                em.op("sync", lambda sy, o=o, tc=tc: sy.dma_start(
                    out=xstage[o][:], in_=acc_sp[o * 128:(o + 1) * 128, tc * 512:(tc + 1) * 512]),
                    sem="accr", k=16)
            for o in range(8):
                def mk_wo(t, i, bank, o=o):
                    return t.matmul(bank[:], wo16[i][:, o * 128:(o + 1) * 128],
                                    arel[i][:], start=(i == 0), stop=(i == 7))

                def ev_wo(bank, o=o, tc=tc):
                    par = o % 2
                    em.wait("vector", "out", st["outst_free"][par])
                    em.wait("vector", "accr", em.cnt("accr"))
                    em.op("vector", lambda v: v.tensor_add(outst[par][:], xstage[o][:], bank[:]),
                          sem="ev")
                    em.wait("gpsimd", "ev", em.cnt("ev"))
                    em.op("gpsimd", lambda gp: gp.dma_start(
                        out=outT[o * 128:(o + 1) * 128, tc * 512:(tc + 1) * 512],
                        in_=outst[par][:]), sem="out", k=16)
                    st["outst_free"][par] = em.cnt("out")

                em.wait("tensor", "arel", em.cnt("arel"))
                acc_tile(8, mk_wo, ev_wo)

        em.wait("gpsimd", "out", em.cnt("out"))
        em.wait("sync", "out", em.cnt("out"))
        if debug:
            em.op("sync", lambda sy: sy.dma_start(out=kT_dbg[:], in_=kT2[(NG - 1) % 2][:]), sem="arel", k=16)
            em.op("sync", lambda sy: sy.dma_start(out=qT_dbg[:], in_=qT[:]), sem="arel", k=16)
            em.op("sync", lambda sy: sy.dma_start(out=vg_dbg[:], in_=vg[:]), sem="arel", k=16)
            em.op("sync", lambda sy: sy.dma_start(out=rr_dbg[:], in_=rrowT[:]), sem="arel", k=16)
            em.op("sync", lambda sy: sy.dma_start(out=p16_dbg[:], in_=p16[0][1][:]), sem="arel", k=16)
            em.wait("sync", "arel", em.cnt("arel"))


    with nc.Block() as blk:
        final_counts = {}

        def run(engine_name):
            def f(eng):
                em = Em(engine_name, eng, sems)
                walk(em)
                if final_counts:
                    assert final_counts == sems.n, (
                        "sem count divergence across walks", engine_name,
                        {k: (final_counts[k], sems.n[k]) for k in sems.n
                         if final_counts[k] != sems.n[k]})
                else:
                    final_counts.update(sems.n)
            return f

        blk.sync(run("sync"))
        blk.vector(run("vector"))
        blk.scalar(run("scalar"))
        blk.tensor(run("tensor"))
        blk.gpsimd(run("gpsimd"))

    lower_extended_insts(nc)
    return nc


_NC_CACHE = None


def _get_nc():
    global _NC_CACHE
    if _NC_CACHE is None:
        _NC_CACHE = build_core_program()
    return _NC_CACHE


def _tile_cols(w):
    return np.ascontiguousarray(
        w.reshape(8, 128, 8, 128).transpose(2, 1, 0, 3).reshape(1024, 1024))


def _hadamard_bf16():
    h = np.array([[1.0]], dtype=np.float64)
    while h.shape[0] < D:
        h = np.block([[h, h], [h, -h]])
    return (h / np.sqrt(D)).astype(np.float32).astype(ml_dtypes.bfloat16)


def kernel(L0_attn_x, L0_attn_Wq, L0_attn_Wk, L0_attn_Wv, L0_attn_Wo,
           L0_attn_bo, L0_ffn_weight):
    x = np.asarray(L0_attn_x, dtype=np.float32)
    wq = _tile_cols(np.asarray(L0_attn_Wq, dtype=np.float32))
    wk = _tile_cols(np.asarray(L0_attn_Wk, dtype=np.float32))
    wv = _tile_cols(np.asarray(L0_attn_Wv, dtype=np.float32))
    wo = np.ascontiguousarray(np.asarray(L0_attn_Wo, dtype=np.float32))
    bo = np.asarray(L0_attn_bo, dtype=np.float32)
    wf = np.asarray(L0_ffn_weight, dtype=np.float32)

    boc = np.ascontiguousarray(bo.reshape(8, 128).T)
    wfT = np.ascontiguousarray(wf.T)
    h16 = np.ascontiguousarray(
        _hadamard_bf16().reshape(8, 128, 8, 128).transpose(2, 1, 0, 3)
        .reshape(1024, 1024))

    nc = _get_nc()
    in_maps = []
    for core in range(8):
        b, qh = core // 2, core % 2
        xt = np.ascontiguousarray(np.roll(x[b].T, -qh * QL, axis=1))
        in_maps.append({
            "xT": xt, "wq": wq, "wk": wk, "wv": wv, "wo": wo,
            "boc": boc, "wfT": wfT, "h16d": h16,
        })
    res = run_bass_kernel_spmd(nc, in_maps, core_ids=list(range(8)))
    out = np.empty((4, S, D), dtype=np.float32)
    for core in range(8):
        b, qh = core // 2, core % 2
        out[b, qh * QL:(qh + 1) * QL, :] = res.results[core]["outT"].T
    return out


# version 19
# speedup vs baseline: 1.0372x; 1.0372x over previous
import numpy as np
import ml_dtypes
import concourse.bass as bass
import concourse.mybir as mybir
from concourse.bass_utils import run_bass_kernel_spmd
from concourse.bass_isa import ReduceOp
from concourse.library_config import attn as attn_lib
from concourse.library_overlay import lower_extended_insts

F32 = mybir.dt.float32
BF16 = mybir.dt.bfloat16
I32 = mybir.dt.int32
AF = mybir.ActivationFunctionType
OP = mybir.AluOpType
AX = mybir.AxisListType

D = 1024
S = 4096
QL = 2048
NB = 8
NG = 8
NQC = 4
NKC = 32
NKG = 16
MAGIC_EPS = 1e-5

ENGINES = ("sync", "vector", "scalar", "tensor", "gpsimd")


class Sems:

    NAMES = ("x", "xg", "xc", "xq", "fin", "sd", "rs", "bc", "xn", "wd", "wg", "wc",
             "ps", "ev", "q16", "mxv", "mxg", "acst", "xrel",
             "qk", "exp", "pvp", "rcp", "rcb", "a16", "mlt", "asp",
             "arel", "accr", "out")

    def __init__(self, nc):
        self.dummies = [nc.alloc_semaphore(f"s_dummy{i}") for i in range(24)]
        self.h = {n: nc.alloc_semaphore("s_" + n) for n in self.NAMES}
        self.reset()

    def reset(self):
        self.n = {k: 0 for k in self.NAMES}


class Em:

    def __init__(self, engine_name, eng, sems):
        self.who = engine_name
        self.eng = eng
        self.s = sems

    def op(self, who, fn, sem=None, k=1):
        if who == self.who:
            ins = fn(self.eng)
            if sem is not None:
                ins.then_inc(self.s.h[sem], k)
        if sem is not None:
            self.s.n[sem] += k

    def wait(self, who, sem, val=None):
        if who == self.who:
            v = self.s.n[sem] if val is None else val
            if v > 0:
                self.eng.wait_ge(self.s.h[sem], v)

    def cnt(self, sem):
        return self.s.n[sem]


def build_core_program(debug=False):
    nc = bass.Bass("TRN2", target_bir_lowering=False, detect_race_conditions=False)
    dbg_kind = "ExternalOutput" if debug else "Internal"

    xT = nc.dram_tensor("xT", [D, S], F32, kind="ExternalInput")
    wq = nc.dram_tensor("wq", [D, D], F32, kind="ExternalInput")
    wk = nc.dram_tensor("wk", [D, D], F32, kind="ExternalInput")
    wv = nc.dram_tensor("wv", [D, D], F32, kind="ExternalInput")
    wo = nc.dram_tensor("wo", [D, D], F32, kind="ExternalInput")
    boc = nc.dram_tensor("boc", [128, 8], F32, kind="ExternalInput")
    wfT = nc.dram_tensor("wfT", [D, D], F32, kind="ExternalInput")
    h16d = nc.dram_tensor("h16d", [D, D], BF16, kind="ExternalInput")
    outT = nc.dram_tensor("outT", [D, QL], F32, kind="ExternalOutput")

    acc_sp = nc.dram_tensor("acc_sp", [D, QL], F32, kind=dbg_kind)
    a_sp = nc.dram_tensor("a_sp", [D, QL], BF16, kind=dbg_kind)
    xn_dbg = nc.dram_tensor("xn_dbg", [D, S], BF16, kind=dbg_kind) if debug else None
    if debug:
        kT_dbg = nc.dram_tensor("kT_dbg", [128, S], BF16, kind=dbg_kind)
        qT_dbg = nc.dram_tensor("qT_dbg", [128, QL], BF16, kind=dbg_kind)
        vg_dbg = nc.dram_tensor("vg_dbg", [128, NKC * 130], BF16, kind=dbg_kind)
        rr_dbg = nc.dram_tensor("rr_dbg", [128, 512], F32, kind=dbg_kind)
        p16_dbg = nc.dram_tensor("p16_dbg", [128, 1024], BF16, kind=dbg_kind)
        xh_dbg = nc.dram_tensor("xh_dbg", [D, 512], F32, kind=dbg_kind)
        hb_dbg = nc.dram_tensor("hb_dbg", [128, 2048], BF16, kind=dbg_kind)
        mx_dbg = nc.dram_tensor("mx_dbg", [128, 2048], F32, kind=dbg_kind)
        q_dbg = nc.dram_tensor("q_dbg", [D, 512], BF16, kind=dbg_kind)

    xn16 = [nc.alloc_sbuf_tensor(f"xn16_{c}", [128, S], BF16) for c in range(8)]
    kT2 = [nc.alloc_sbuf_tensor(f"kT{i}", [128, S], BF16) for i in range(2)]
    vg = nc.alloc_sbuf_tensor("vg", [128, NKC * 130], BF16)
    qT = nc.alloc_sbuf_tensor("qT", [128, QL], BF16)
    p16 = [[nc.alloc_sbuf_tensor(f"p16_{h}_{b}", [128, 1024], BF16)
            for b in range(2)] for h in range(2)]
    asta1 = [nc.alloc_sbuf_tensor(f"asta_{h}", [64, 512], BF16) for h in range(2)]
    rrowT = nc.alloc_sbuf_tensor("rrowT", [128, 512], F32)
    rrow0 = nc.alloc_sbuf_tensor("rrow0", [1, 512], F32)
    rcb = [nc.alloc_sbuf_tensor(f"rcb{i}", [64, 512], F32) for i in range(2)]
    wst = [nc.alloc_sbuf_tensor(f"wst{i}", [128, 1024], F32) for i in range(2)]
    w16 = {m: nc.alloc_sbuf_tensor(f"w16{m}", [128, 1024], BF16) for m in "qkv"}
    ones16 = nc.alloc_sbuf_tensor("ones16", [128, 128], BF16)
    bocs = nc.alloc_sbuf_tensor("bocs", [128, 8], F32)
    xstage = [nc.alloc_sbuf_tensor(f"xstage{c}", [128, 512], F32) for c in range(8)]

    REGION = 66 * 1024
    arena = nc.alloc_sbuf_tensor("arena", [128, REGION], mybir.dt.uint8)
    arena_addr = nc.lookup_mloc(arena).addr

    def overlay(name, shape, dtype, off):
        per_part = int(np.prod(shape[1:])) * mybir.dt.size(dtype)
        assert off + per_part <= REGION, (name, off, per_part)
        h = nc.virtual_sbuf_tensor(name, shape, dtype)
        nc._tensor(h.name, list(shape), dtype, type="SB")
        ml = nc.lookup_mloc(h)
        ml.allocated = not nc.target_bir_lowering
        ml.addr = arena_addr + off
        return h

    o = 0
    tern16 = []
    for c in range(8):
        tern16.append(overlay(f"tern16_{c}", [128, D], BF16, o)); o += 2048
    xh32 = []
    for j in range(8):
        xh32.append(overlay(f"xh32_{j}", [128, 512], F32, o)); o += 2048
    q16t = []
    for j in range(8):
        q16t.append(overlay(f"q16t_{j}", [128, 512], BF16, 16384 + j * 2048))
    hbufB = []
    for p in range(8):
        hbufB.append(overlay(f"hbufB{p}", [128, 1024], BF16, o)); o += 2048
    O_MX = o
    mx1 = overlay("mx1", [128, 512], F32, o); o += 2048
    mx2 = overlay("mx2", [128, 512], F32, o); o += 2048
    r7B = overlay("r7B", [128, 512], F32, o); o += 2048
    srB = overlay("srB", [128, 512], F32, o); o += 2048
    wmB = overlay("wmB", [128, D], F32, o)
    fi32 = overlay("fi32", [128, 512], I32, o); o += 2048
    ff1 = overlay("ff1", [128, 512], F32, o); o += 2048
    ff2 = overlay("ff2", [128, 512], F32, o); o += 2048
    r_wm = overlay("r_wm", [1, D], F32, 16384 + 4 * 2048)
    r_rw = overlay("r_rw", [1, D], F32, 16384 + 6 * 2048)
    rwB = overlay("rwB", [128, D], F32, O_MX)
    F_END = o
    assert F_END <= REGION

    o = 0
    xb16 = []
    for c in range(8):
        xb16.append(overlay(f"xb16_{c}", [128, 512], BF16, o)); o += 1024
    xsq16 = []
    for c in range(8):
        xsq16.append(overlay(f"xsq16_{c}", [128, 512], BF16, o)); o += 1024
    xodd = []
    for c in range(8):
        xodd.append(overlay(f"xodd_{c}", [128, 512], F32, o)); o += 2048
    r_mean = [overlay("r_mean0", [1, 512], F32, o), overlay("r_mean1", [1, 512], F32, o + 2048)]
    o += 4096
    r_ms = [overlay("r_ms0", [1, 512], F32, o), overlay("r_ms1", [1, 512], F32, o + 2048)]
    o += 4096
    r_c = [overlay("r_c0", [1, 512], F32, o), overlay("r_c1", [1, 512], F32, o + 2048)]
    o += 4096
    meanB = [overlay("meanB0", [128, 512], F32, o), overlay("meanB1", [128, 512], F32, o + 2048)]
    o += 4096
    rsB = [overlay("rsB0", [128, 512], F32, o), overlay("rsB1", [128, 512], F32, o + 2048)]
    o += 4096
    tA = [overlay("tA", [128, 512], F32, o)] * 2
    o += 2048
    assert o <= REGION

    o = 0
    wo16 = []
    for c in range(8):
        wo16.append(overlay(f"wo16_{c}", [128, D], BF16, o)); o += 2048
    arel = []
    for c in range(8):
        arel.append(overlay(f"arel{c}", [128, 512], BF16, o)); o += 1024
    outst = [overlay("outst0", [128, 512], F32, o), overlay("outst1", [128, 512], F32, o + 2048)]
    o += 4096
    D_END = o
    p16x = [overlay(f"p16x{h}", [128, 1024], BF16, D_END + h * 2048) for h in range(2)]
    av65 = [overlay(f"av65_{h}", [65, 512], F32, D_END + 4096 + h * 2048) for h in range(2)]
    qT2b = overlay("qT2b", [128, 2048], BF16, D_END + 8192)
    assert o <= REGION

    fost = [mx1, mx2]

    pbS = [nc.alloc_psum_tensor(f"pbS{i}", [128, 1024], F32) for i in range(2)]
    pbA2 = [[nc.alloc_psum_tensor(f"pbA{h}_{p}", [128, 512], F32) for p in range(2)]
            for h in range(2)]
    pbT = [pbS[0][:, 0:512], pbS[1][:, 0:512]]

    for h in range(2):
        p16[h].append(p16x[h])
    qT2 = [qT, qT2b]
    sems = Sems(nc)

    def walk(em: Em):
        sems.reset()
        st = {"tile": 0, "ev_snap": [],
              "sbank_free": [0, 0],
              "pbuf_free": [[0, 0, 0], [0, 0, 0]],
              "pvpsum_free": [0, 0],
              "w16_free": 0,
              "rcb_free": [0, 0],
              "av_free": [0, 0],
              "rrow_free": [0, 0],
              "asta_free": 0,
              "outst_free": [0, 0],
              }

        def acc_tile(n_mm, mk_mm, evac, banks=None, te_pre=None):
            k = st["tile"]
            st["tile"] += 1
            bank = (banks or pbT)[k % 2]
            if k >= 2:
                em.wait("tensor", "ev", st["ev_snap"][k - 2] + 1)
            if te_pre is not None:
                te_pre()
            for i in range(n_mm):
                em.op("tensor", lambda t, i=i, bank=bank: mk_mm(t, i, bank),
                      sem="ps" if i == n_mm - 1 else None)
            em.wait("vector", "ps", em.cnt("ps"))
            st["ev_snap"].append(em.cnt("ev"))
            evac(bank)

        em.op("gpsimd", lambda g: g.load_library(attn_lib))

        em.op("vector", lambda v: v.memset(ones16[:], 1.0))
        for pb in (pbS[0], pbS[1], pbA2[0][0], pbA2[0][1], pbA2[1][0], pbA2[1][1]):
            em.op("vector", lambda v, pb=pb: v.memset(pb[:], 0.0))
        if debug:
            em.op("vector", lambda v: v.memset(hbufB[0][:], 777.0))
            em.op("vector", lambda v: v.memset(hbufB[1][:], 777.0))
        vgr = vg[:].rearrange("p (t s c) -> p t s c", t=NKC, s=2, c=65)
        em.op("vector", lambda v: v.memset(vgr[:, :, :, 64:65], 1.0))
        em.op("sync", lambda sy: sy.dma_start(out=bocs[:], in_=boc[:]), sem="wd", k=16)

        for tb in range(NB):
            par = tb % 2
            xbl = xstage if par == 0 else xodd
            if tb >= 2:
                em.wait("sync", "xn", tb - 1)
                em.wait("gpsimd", "xn", tb - 1)
            em.wait("sync", "x", em.cnt("x"))
            em.wait("gpsimd", "xg", em.cnt("xg"))
            for c in range(8):
                who = "sync" if c < 4 else "gpsimd"
                em.op(who, lambda e, c=c: e.dma_start(
                    out=xbl[c][:], in_=xT[c * 128:(c + 1) * 128, tb * 512:(tb + 1) * 512]),
                    sem="x" if c < 4 else "xg", k=16)
            em.wait("vector", "x", em.cnt("x"))
            em.wait("vector", "xg", em.cnt("xg"))
            em.wait("scalar", "x", em.cnt("x"))
            em.wait("scalar", "xg", em.cnt("xg"))
            for c in range(8):
                em.op("vector", lambda v, c=c: v.tensor_copy(out=xb16[c][:], in_=xbl[c][:]),
                      sem="xc" if c == 7 else None)
                em.op("scalar", lambda sc, c=c: sc.activation(
                    out=xsq16[c][:], in_=xbl[c][:], func=AF.Square),
                    sem="xq" if c == 7 else None)

            def mk_sum(t, i, bank, src):
                return t.matmul(bank[:], ones16[:], src[i][:],
                                start=(i == 0), stop=(i == 7))

            def ev_sum(bank, dst):
                em.op("vector", lambda v: v.tensor_scalar(
                    out=dst[:], in0=bank[0:1, :],
                    scalar1=1.0 / 1024.0, scalar2=None, op0=OP.mult), sem="ev")

            em.wait("tensor", "xc", em.cnt("xc"))
            em.wait("tensor", "xq", em.cnt("xq"))
            acc_tile(8, lambda t, i, bank: mk_sum(t, i, bank, xb16),
                     lambda bank: ev_sum(bank, r_mean[par]))
            acc_tile(8, lambda t, i, bank: mk_sum(t, i, bank, xsq16),
                     lambda bank: ev_sum(bank, r_ms[par]))
            em.wait("vector", "ev", em.cnt("ev"))
            em.op("vector", lambda v: v.tensor_mul(
                r_c[par][:], r_mean[par][:], r_mean[par][:]))
            em.op("vector", lambda v: v.tensor_sub(
                r_ms[par][:], r_ms[par][:], r_c[par][:]))
            em.op("vector", lambda v: v.tensor_scalar_add(
                r_ms[par][:], r_ms[par][:], MAGIC_EPS), sem="fin")
            em.wait("scalar", "fin", em.cnt("fin"))
            em.op("scalar", lambda sc: sc.activation(
                out=r_c[par][:], in_=r_ms[par][:], func=AF.Sqrt), sem="sd")
            em.wait("vector", "sd", em.cnt("sd"))
            em.op("vector", lambda v: v.reciprocal(
                out=r_c[par][:], in_=r_c[par][:]), sem="rs")
            em.wait("gpsimd", "rs", em.cnt("rs"))
            if tb >= 2:
                em.wait("gpsimd", "xn", tb - 1)
            em.op("gpsimd", lambda g: g.partition_broadcast(
                meanB[par][:], r_mean[par][:], 128))
            em.op("gpsimd", lambda g: g.partition_broadcast(
                rsB[par][:], r_c[par][:], 128), sem="bc")
            em.wait("vector", "bc", em.cnt("bc"))
            for c in range(8):
                em.op("vector", lambda v, c=c: v.tensor_sub(tA[par][:], xbl[c][:], meanB[par][:]))
                em.op("vector", lambda v, c=c: v.tensor_mul(
                    xn16[c][:, tb * 512:(tb + 1) * 512], tA[par][:], rsB[par][:]),
                    sem="xn" if c == 7 else None)

        if debug:
            em.wait("gpsimd", "xn", NB)
            for c in range(8):
                em.op("gpsimd", lambda gp, c=c: gp.dma_start(
                    out=xn_dbg[c * 128:(c + 1) * 128, :], in_=xn16[c][:]), sem="out", k=16)

        for c in range(8):
            em.wait("sync", "wc", em.cnt("wc") - 1)
            em.op("sync", lambda sy, c=c: sy.dma_start(
                out=wst[c % 2][:], in_=wfT[c * 128:(c + 1) * 128, :]), sem="wd", k=16)
            em.wait("scalar", "wd", em.cnt("wd"))
            em.op("scalar", lambda sc, c=c: sc.activation(
                out=tern16[c][:], in_=wst[c % 2][:], func=AF.Abs), sem="wc")

        def mk_wsum(t, i, bank, half):
            return t.matmul(bank[:], ones16[:],
                            tern16[i][:, half * 512:(half + 1) * 512],
                            start=(i == 0), stop=(i == 7))

        for half in range(2):
            def ev_wsum(bank, half=half):
                em.op("vector", lambda v: v.tensor_scalar(
                    out=r_wm[0:1, half * 512:(half + 1) * 512], in0=bank[0:1, :],
                    scalar1=1.0 / 1024.0, scalar2=None, op0=OP.mult), sem="ev")
            em.wait("tensor", "wc", em.cnt("wc"))
            acc_tile(8, lambda t, i, bank, half=half: mk_wsum(t, i, bank, half), ev_wsum)
        em.op("vector", lambda v: v.reciprocal(out=r_rw[:], in_=r_wm[:]), sem="rs")
        em.wait("gpsimd", "rs", em.cnt("rs"))
        em.op("gpsimd", lambda g: g.partition_broadcast(rwB[:], r_rw[:], 128))
        em.op("gpsimd", lambda g: g.partition_broadcast(wmB[:], r_wm[:], 128), sem="bc")

        for c in range(8):
            em.wait("sync", "wc", em.cnt("wc") - 1)
            em.op("sync", lambda sy, c=c: sy.dma_start(
                out=wst[c % 2][:], in_=wfT[c * 128:(c + 1) * 128, :]), sem="wd", k=16)
            em.wait("vector", "wd", em.cnt("wd"))
            em.wait("vector", "bc", em.cnt("bc"))
            em.op("vector", lambda v, c=c: v.tensor_mul(xh32[0][:], wst[c % 2][:, 0:512], rwB[:, 0:512]))
            em.op("vector", lambda v, c=c: v.tensor_mul(xh32[1][:], wst[c % 2][:, 512:1024], rwB[:, 512:1024]))
            for half in range(2):
                em.op("vector", lambda v, c=c, half=half: v.tensor_scalar(
                    out=xh32[2][:], in0=xh32[half][:], scalar1=0.5, scalar2=None, op0=OP.is_gt))
                em.op("vector", lambda v, c=c, half=half: v.scalar_tensor_tensor(
                    out=xh32[2][:], in0=xh32[half][:], scalar=-0.5, in1=xh32[2][:],
                    op0=OP.is_ge, op1=OP.add))
                em.op("vector", lambda v, c=c, half=half: v.tensor_scalar(
                    out=xh32[2][:], in0=xh32[2][:],
                    scalar1=1.0, scalar2=None, op0=OP.subtract))
                em.op("vector", lambda v, c=c, half=half: v.tensor_mul(
                    tern16[c][:, half * 512:(half + 1) * 512], xh32[2][:],
                    wmB[:, half * 512:(half + 1) * 512]),
                    sem="wc" if half == 1 else None)

        em.wait("sync", "xn", NB)
        for j in range(8):
            em.op("sync", lambda sy, j=j: sy.dma_start(
                out=hbufB[j][:], in_=h16d[j * 128:(j + 1) * 128, :]),
                sem="wg", k=16)
        for tc in range(NQC):
            if tc >= 1:
                em.wait("sync", "acst", em.cnt("acst"))
            for c in range(8):
                em.op("sync", lambda sy, c=c, tc=tc: sy.dma_start(
                    out=xstage[c][:], in_=xT[c * 128:(c + 1) * 128, tc * 512:(tc + 1) * 512]),
                    sem="xrel", k=16)
            if tc == 0:
                em.wait("tensor", "wg", em.cnt("wg"))
            for j in range(8):
                def mk_xh(t, i, bank, j=j, tc=tc):
                    return t.matmul(bank[:], hbufB[j][:, i * 128:(i + 1) * 128],
                                    xn16[i][:, tc * 512:(tc + 1) * 512],
                                    start=(i == 0), stop=(i == 7))

                def ev_xh(bank, j=j):
                    em.op("vector", lambda v: v.tensor_copy(out=xh32[j][:], in_=bank[:]))
                    if j == 0:
                        em.wait("vector", "acst", em.cnt("acst"))
                        em.op("vector", lambda v: v.tensor_copy(out=mx1[:], in_=xh32[j][:]))
                        em.op("vector", lambda v: v.tensor_copy(out=mx2[:], in_=xh32[j][:]), sem="ev")
                    else:
                        em.op("vector", lambda v: v.tensor_max(mx1[:], mx1[:], xh32[j][:]))
                        em.op("vector", lambda v: v.tensor_tensor(mx2[:], mx2[:], xh32[j][:], OP.min),
                              sem="ev")
                    if j == 7:
                        em.op("vector", lambda v: v.tensor_scalar_mul(mx2[:], mx2[:], -1.0))
                        em.op("vector", lambda v: v.tensor_max(mx1[:], mx1[:], mx2[:]), sem="mxv")

                acc_tile(8, mk_xh, ev_xh, banks=[pbA2[0][0][:], pbA2[0][1][:]])

            em.wait("gpsimd", "mxv", em.cnt("mxv"))
            if debug and tc == 0:
                em.op("gpsimd", lambda g: g.dma_start(
                    out=mx_dbg[:, 1536:2048], in_=mx1[:]), sem="asp", k=16)
            em.op("gpsimd", lambda g: g.partition_all_reduce(srB[:], mx1[:], 128, ReduceOp.max),
                  sem="mxg")
            em.wait("vector", "mxg", em.cnt("mxg"))
            em.op("vector", lambda v: v.tensor_scalar_max(mx2[:], srB[:], MAGIC_EPS))
            em.op("vector", lambda v: v.reciprocal(out=r7B[:], in_=mx2[:]))
            em.op("vector", lambda v: v.tensor_scalar_mul(r7B[:], r7B[:], 7.0))
            em.op("vector", lambda v: v.tensor_scalar_mul(srB[:], mx2[:], 1.0 / 7.0))
            for j in range(8):
                em.op("vector", lambda v, j=j: v.tensor_mul(ff1[:], xh32[j][:], r7B[:]))
                em.op("vector", lambda v: v.tensor_copy(out=fi32[:], in_=ff1[:]))
                em.op("vector", lambda v: v.tensor_copy(out=ff2[:], in_=fi32[:]))
                em.op("vector", lambda v: v.tensor_sub(ff1[:], ff1[:], ff2[:]))
                em.op("vector", lambda v: v.scalar_tensor_tensor(
                    out=ff2[:], in0=ff1[:], scalar=0.5, in1=ff2[:], op0=OP.is_gt, op1=OP.add))
                em.op("vector", lambda v: v.scalar_tensor_tensor(
                    out=ff2[:], in0=ff1[:], scalar=-0.5, in1=ff2[:], op0=OP.is_ge, op1=OP.add))
                em.op("vector", lambda v: v.tensor_scalar(
                    out=ff2[:], in0=ff2[:], scalar1=1.0, scalar2=7.0, op0=OP.subtract, op1=OP.min))
                em.op("vector", lambda v, j=j: v.tensor_scalar_max(q16t[j][:], ff2[:], -8.0),
                      sem="q16" if j == 7 else None)

            if debug and tc == 0:
                em.wait("gpsimd", "q16", em.cnt("q16"))
                for j in range(8):
                    em.op("gpsimd", lambda gp, j=j: gp.dma_start(
                        out=xh_dbg[j * 128:(j + 1) * 128, :], in_=xh32[j][:]), sem="asp", k=16)
                    em.op("gpsimd", lambda gp, j=j: gp.dma_start(
                        out=q_dbg[j * 128:(j + 1) * 128, :], in_=q16t[j][:]), sem="asp", k=16)
                em.op("gpsimd", lambda gp: gp.dma_start(out=hb_dbg[:, 0:1024], in_=hbufB[0][:]), sem="asp", k=16)
                em.op("gpsimd", lambda gp: gp.dma_start(out=hb_dbg[:, 1024:2048], in_=hbufB[1][:]), sem="asp", k=16)
                em.op("gpsimd", lambda gp: gp.dma_start(out=mx_dbg[:, 0:512], in_=mx1[:]), sem="asp", k=16)
                em.op("gpsimd", lambda gp: gp.dma_start(out=mx_dbg[:, 512:1024], in_=r7B[:]), sem="asp", k=16)
                em.op("gpsimd", lambda gp: gp.dma_start(out=mx_dbg[:, 1024:1536], in_=srB[:]), sem="asp", k=16)
                em.wait("tensor", "asp", em.cnt("asp"))

            for o in range(8):
                def mk_ffn(t, i, bank, o=o):
                    return t.matmul(bank[:], tern16[i][:, o * 128:(o + 1) * 128],
                                    q16t[i][:], start=(i == 0), stop=(i == 7))

                def ev_ffn(bank, o=o, tc=tc):
                    par = o % 2
                    em.wait("vector", "acst", st["outst_free"][par])
                    em.op("vector", lambda v: v.tensor_mul(ff1[:], bank[:], srB[:]))
                    em.wait("vector", "xrel", em.cnt("xrel"))
                    em.op("vector", lambda v: v.tensor_scalar_add(ff2[:], xstage[o][:], bocs[:, o:o + 1]))
                    em.op("vector", lambda v: v.tensor_add(fost[par][:], ff1[:], ff2[:]), sem="ev")
                    em.wait("gpsimd", "ev", em.cnt("ev"))
                    em.op("gpsimd", lambda g: g.dma_start(
                        out=acc_sp[o * 128:(o + 1) * 128, tc * 512:(tc + 1) * 512],
                        in_=fost[par][:]), sem="acst", k=16)
                    st["outst_free"][par] = em.cnt("acst")

                em.wait("tensor", "q16", em.cnt("q16"))
                em.wait("tensor", "wc", em.cnt("wc"))
                acc_tile(8, mk_ffn, ev_ffn, banks=[pbA2[1][0][:], pbA2[1][1][:]])

        def load_w_group(g):
            for mi, (mat, wdr) in enumerate((("q", wq), ("k", wk), ("v", wv))):
                em.wait("sync", "wc", em.cnt("wc"))
                em.op("sync", lambda sy, wdr=wdr, g=g, mi=mi: sy.dma_start(
                    out=wst[mi % 2][:], in_=wdr[g * 128:(g + 1) * 128, :]),
                    sem="wd", k=16)
                em.wait("vector", "wd", em.cnt("wd"))
                em.wait("vector", "ps", st["w16_free"])
                em.op("vector", lambda v, mat=mat, mi=mi: v.tensor_copy(
                    out=w16[mat][:], in_=wst[mi % 2][:]), sem="wc")

        def emit_k_tiles(g, banks, te_pre_first):
            for t8 in range(8):
                def mk_k(t, i, bank, t8=t8, g=g):
                    return t.matmul(bank[:], w16["k"][:, i * 128:(i + 1) * 128],
                                    xn16[i][:, t8 * 512:(t8 + 1) * 512],
                                    start=(i == 0), stop=(i == 7))

                def ev_k(bank, t8=t8, g=g):
                    em.op("vector", lambda v: v.tensor_copy(
                        out=kT2[g % 2][:, t8 * 512:(t8 + 1) * 512], in_=bank[:]), sem="ev")
                acc_tile(8, mk_k, ev_k, banks=banks,
                         te_pre=te_pre_first if t8 == 0 else None)

        load_w_group(0)
        for g in range(NG):
            em.wait("tensor", "xn", NB)
            em.wait("tensor", "wc", em.cnt("wc"))
            em.wait("tensor", "exp", em.cnt("exp"))
            if g == 0:
                for tq in range(NQC):
                    def mk_q(t, i, bank, tq=tq):
                        return t.matmul(bank[:], w16["q"][:, i * 128:(i + 1) * 128],
                                        xn16[i][:, tq * 512:(tq + 1) * 512],
                                        start=(i == 0), stop=(i == 7))

                    def ev_q(bank, tq=tq):
                        em.op("vector", lambda v: v.tensor_scalar(
                            out=qT2[0][:, tq * 512:(tq + 1) * 512], in0=bank[:],
                            scalar1=0.125, scalar2=None, op0=OP.mult), sem="ev")
                    acc_tile(8, mk_q, ev_q)
            if g == 0:
                emit_k_tiles(0, pbT, None)
            def emit_v_tile(tt):
                def mk_v(t, i, bank, tt=tt):
                    return t.matmul(bank[:, 0:128], xn16[i][:, tt * 128:(tt + 1) * 128],
                                    w16["v"][:, i * 128:(i + 1) * 128],
                                    start=(i == 0), stop=(i == 7))

                def ev_v(bank, tt=tt):
                    em.op("vector", lambda v: v.tensor_copy(
                        out=vgr[:, tt, :, 0:64],
                        in_=bank[:, 0:128].rearrange("p (s c) -> p s c", s=2)), sem="ev")

                def pre_v(tt=tt):
                    if tt < 2:
                        em.wait("tensor", "a16", st["pvpsum_free"][1])
                acc_tile(8, mk_v, ev_v, banks=[pbA2[0][1][:], pbA2[1][1][:]], te_pre=pre_v)
                st["v_ready"][tt] = em.cnt("ev")

            st["v_ready"] = [0] * NKC
            st.setdefault("k_ev_snap", 0)

            for qc in range(NQC):
                apar = qc % 2
                exp_done = [[0] * NKG for _ in range(2)]
                for kg in range(NKG + 2):
                    if qc == 0 and kg == 0:
                        st["qk_ev_snap"] = em.cnt("ev")
                    if qc == 2 and kg % 2 == 0 and kg < NKG and g + 1 < NG:
                        t8 = kg // 2

                        def mk_k2(t, i, bank, t8=t8, gg=g + 1):
                            return t.matmul(bank[:], w16["k"][:, i * 128:(i + 1) * 128],
                                            xn16[i][:, t8 * 512:(t8 + 1) * 512],
                                            start=(i == 0), stop=(i == 7))

                        def ev_k2(bank, t8=t8, gg=g + 1):
                            em.op("vector", lambda v: v.tensor_copy(
                                out=kT2[gg % 2][:, t8 * 512:(t8 + 1) * 512],
                                in_=bank[:]), sem="ev")

                        def pre_k2(t8=t8):
                            if t8 == 0:
                                em.wait("tensor", "a16", st["pvpsum_free"][1])
                        acc_tile(8, mk_k2, ev_k2,
                                 banks=[pbA2[0][1][:], pbA2[1][1][:]], te_pre=pre_k2)
                        st["k_ev_snap"] = em.cnt("ev")
                    if qc == 3 and kg % 2 == 1 and kg < 8 and g + 1 < NG:
                        tq = kg // 2

                        def mk_q2(t, i, bank, tq=tq):
                            return t.matmul(bank[:], w16["q"][:, i * 128:(i + 1) * 128],
                                            xn16[i][:, tq * 512:(tq + 1) * 512],
                                            start=(i == 0), stop=(i == 7))

                        def ev_q2(bank, tq=tq, gg=g + 1):
                            em.op("vector", lambda v: v.tensor_scalar(
                                out=qT2[gg % 2][:, tq * 512:(tq + 1) * 512], in0=bank[:],
                                scalar1=0.125, scalar2=None, op0=OP.mult), sem="ev")

                        def pre_q2(tq=tq):
                            if tq == 0:
                                em.wait("tensor", "a16", st["pvpsum_free"][0])
                                em.wait("tensor", "wc", em.cnt("wc"))
                        acc_tile(8, mk_q2, ev_q2,
                                 banks=[pbA2[0][0][:], pbA2[1][0][:]], te_pre=pre_q2)
                    if kg < NKG:
                        for hh in range(2):
                            em.wait("tensor", "exp", st["sbank_free"][hh])
                            if qc == 0 and kg == 0 and hh == 0:
                                em.wait("tensor", "ev", st["qk_ev_snap"])
                            for half in range(2):
                                kc = 2 * kg + half
                                em.op("tensor", lambda t, hh=hh, half=half, kc=kc, qc=qc: t.matmul(
                                    pbS[hh][:, half * 512:(half + 1) * 512],
                                    kT2[g % 2][hh * 64:(hh + 1) * 64, kc * 128:(kc + 1) * 128],
                                    qT2[g % 2][hh * 64:(hh + 1) * 64, qc * 512:(qc + 1) * 512],
                                    start=True, stop=True),
                                    sem="qk" if half == 1 else None)
                            em.wait("scalar", "qk", em.cnt("qk"))
                            em.wait("scalar", "pvp", st["pbuf_free"][hh][kg % 3])
                            em.op("scalar", lambda sc, hh=hh, kg=kg: sc.activation(
                                out=p16[hh][kg % 3][:], in_=pbS[hh][:], func=AF.Exp), sem="exp")
                            st["sbank_free"][hh] = em.cnt("exp")
                            exp_done[hh][kg] = em.cnt("exp")
                    if kg >= 2:
                        pk = kg - 2
                        for hh in range(2):
                            em.wait("tensor", "exp", exp_done[hh][pk])
                            if qc == 0 and hh == 0:
                                em.wait("tensor", "ev", st["v_ready"][2 * pk + 1])
                            if pk == 0:
                                em.wait("tensor", "a16", st["pvpsum_free"][apar])
                                if qc == 1 and hh == 0:
                                    em.wait("tensor", "ev", st["v_ready"][NKC - 1])
                                if qc == 3 and hh == 0 and st["k_ev_snap"] > 0:
                                    em.wait("tensor", "ev", st["k_ev_snap"])
                            for half in range(2):
                                kc = 2 * pk + half
                                em.op("tensor", lambda t, hh=hh, half=half, kc=kc, pk=pk, apar=apar: t.matmul(
                                    pbA2[hh][apar][0:65, :],
                                    vgr[:, kc, hh, :],
                                    p16[hh][pk % 3][:, half * 512:(half + 1) * 512],
                                    start=(kc == 0), stop=(kc == NKC - 1)),
                                    sem="pvp" if half == 1 else None)
                            st["pbuf_free"][hh][pk % 3] = em.cnt("pvp")
                    if qc == 0 and kg < NKG:
                        for tt in (2 * kg, 2 * kg + 1):
                            emit_v_tile(tt)
                        if kg == NKG - 1:
                            st["w16_free"] = em.cnt("ps")
                            if g + 1 < NG:
                                load_w_group(g + 1)

                em.wait("vector", "pvp", em.cnt("pvp"))
                for hh in range(2):
                    em.wait("vector", "mlt", st["av_free"][hh])
                    em.op("vector", lambda v, hh=hh, apar=apar: v.tensor_copy(
                        out=av65[hh][:], in_=pbA2[hh][apar][0:65, :]), sem="a16")
                st["pvpsum_free"][apar] = em.cnt("a16")
                st["av_free"] = [em.cnt("mlt") + 1, em.cnt("mlt") + 2]
                for hh in range(2):
                    em.wait("vector", "rcb", st["rrow_free"][hh])
                    em.op("vector", lambda v, hh=hh: v.reciprocal(
                        out=av65[hh][64:65, :], in_=av65[hh][64:65, :]))
                    em.op("vector", lambda v, hh=hh: v.tensor_copy(
                        out=rrow0[:], in_=av65[hh][64:65, :]), sem="rcp")
                    em.wait("gpsimd", "rcp", em.cnt("rcp"))
                    em.wait("gpsimd", "mlt", st["rcb_free"][hh])
                    em.op("gpsimd", lambda gp, hh=hh: gp.partition_broadcast(
                        rcb[hh][:], rrow0[:], 64), sem="rcb")
                    st["rrow_free"][hh] = em.cnt("rcb")
                    st["rcb_free"][hh] = em.cnt("mlt") + 1 + hh
                em.wait("vector", "rcb", em.cnt("rcb"))
                em.wait("vector", "asp", st["asta_free"])
                for hh in range(2):
                    em.op("vector", lambda v, hh=hh: v.tensor_mul(
                        asta1[hh][:], av65[hh][0:64, :], rcb[hh][:]), sem="mlt")
                em.wait("gpsimd", "mlt", em.cnt("mlt"))
                for hh in range(2):
                    em.op("gpsimd", lambda gp, g=g, qc=qc, hh=hh: gp.dma_start(
                        out=a_sp[g * 128 + hh * 64:g * 128 + (hh + 1) * 64,
                                 qc * 512:(qc + 1) * 512],
                        in_=asta1[hh][:]), sem="asp", k=16)
                st["asta_free"] = em.cnt("asp")

        st["outst_free"] = [0, 0]
        for c in range(8):
            em.wait("sync", "wc", em.cnt("wc") - (0 if c < 2 else 1))
            em.op("sync", lambda sy, c=c: sy.dma_start(
                out=wst[c % 2][:], in_=wo[c * 128:(c + 1) * 128, :]), sem="wd", k=16)
            em.wait("vector", "wd", em.cnt("wd"))
            em.op("vector", lambda v, c=c: v.tensor_copy(out=wo16[c][:], in_=wst[c % 2][:]),
                  sem="wc")

        for tc in range(NQC):
            em.wait("sync", "asp", em.cnt("asp"))
            if tc >= 1:
                em.wait("sync", "ps", em.cnt("ps"))
                em.wait("sync", "ev", em.cnt("ev"))
            for hd in range(8):
                em.op("sync", lambda sy, hd=hd, tc=tc: sy.dma_start(
                    out=arel[hd][:], in_=a_sp[hd * 128:(hd + 1) * 128, tc * 512:(tc + 1) * 512]),
                    sem="arel", k=16)
            em.wait("sync", "acst", em.cnt("acst"))
            for o in range(8):
                em.op("sync", lambda sy, o=o, tc=tc: sy.dma_start(
                    out=xstage[o][:], in_=acc_sp[o * 128:(o + 1) * 128, tc * 512:(tc + 1) * 512]),
                    sem="accr", k=16)
            for o in range(8):
                def mk_wo(t, i, bank, o=o):
                    return t.matmul(bank[:], wo16[i][:, o * 128:(o + 1) * 128],
                                    arel[i][:], start=(i == 0), stop=(i == 7))

                def ev_wo(bank, o=o, tc=tc):
                    par = o % 2
                    em.wait("vector", "out", st["outst_free"][par])
                    em.wait("vector", "accr", em.cnt("accr"))
                    em.op("vector", lambda v: v.tensor_add(outst[par][:], xstage[o][:], bank[:]),
                          sem="ev")
                    em.wait("gpsimd", "ev", em.cnt("ev"))
                    em.op("gpsimd", lambda gp: gp.dma_start(
                        out=outT[o * 128:(o + 1) * 128, tc * 512:(tc + 1) * 512],
                        in_=outst[par][:]), sem="out", k=16)
                    st["outst_free"][par] = em.cnt("out")

                em.wait("tensor", "arel", em.cnt("arel"))
                acc_tile(8, mk_wo, ev_wo)

        em.wait("gpsimd", "out", em.cnt("out"))
        em.wait("sync", "out", em.cnt("out"))
        if debug:
            em.op("sync", lambda sy: sy.dma_start(out=kT_dbg[:], in_=kT2[(NG - 1) % 2][:]), sem="arel", k=16)
            em.op("sync", lambda sy: sy.dma_start(out=qT_dbg[:], in_=qT[:]), sem="arel", k=16)
            em.op("sync", lambda sy: sy.dma_start(out=vg_dbg[:], in_=vg[:]), sem="arel", k=16)
            em.op("sync", lambda sy: sy.dma_start(out=rr_dbg[:], in_=rrowT[:]), sem="arel", k=16)
            em.op("sync", lambda sy: sy.dma_start(out=p16_dbg[:], in_=p16[0][1][:]), sem="arel", k=16)
            em.wait("sync", "arel", em.cnt("arel"))


    with nc.Block() as blk:
        final_counts = {}

        def run(engine_name):
            def f(eng):
                em = Em(engine_name, eng, sems)
                walk(em)
                if final_counts:
                    assert final_counts == sems.n, (
                        "sem count divergence across walks", engine_name,
                        {k: (final_counts[k], sems.n[k]) for k in sems.n
                         if final_counts[k] != sems.n[k]})
                else:
                    final_counts.update(sems.n)
            return f

        blk.sync(run("sync"))
        blk.vector(run("vector"))
        blk.scalar(run("scalar"))
        blk.tensor(run("tensor"))
        blk.gpsimd(run("gpsimd"))

    lower_extended_insts(nc)
    return nc


_NC_CACHE = None


def _get_nc():
    global _NC_CACHE
    if _NC_CACHE is None:
        _NC_CACHE = build_core_program()
    return _NC_CACHE


def _tile_cols(w):
    return np.ascontiguousarray(
        w.reshape(8, 128, 8, 128).transpose(2, 1, 0, 3).reshape(1024, 1024))


def _hadamard_bf16():
    h = np.array([[1.0]], dtype=np.float64)
    while h.shape[0] < D:
        h = np.block([[h, h], [h, -h]])
    return (h / np.sqrt(D)).astype(np.float32).astype(ml_dtypes.bfloat16)


def kernel(L0_attn_x, L0_attn_Wq, L0_attn_Wk, L0_attn_Wv, L0_attn_Wo,
           L0_attn_bo, L0_ffn_weight):
    x = np.asarray(L0_attn_x, dtype=np.float32)
    wq = _tile_cols(np.asarray(L0_attn_Wq, dtype=np.float32))
    wk = _tile_cols(np.asarray(L0_attn_Wk, dtype=np.float32))
    wv = _tile_cols(np.asarray(L0_attn_Wv, dtype=np.float32))
    wo = np.ascontiguousarray(np.asarray(L0_attn_Wo, dtype=np.float32))
    bo = np.asarray(L0_attn_bo, dtype=np.float32)
    wf = np.asarray(L0_ffn_weight, dtype=np.float32)

    boc = np.ascontiguousarray(bo.reshape(8, 128).T)
    wfT = np.ascontiguousarray(wf.T)
    h16 = np.ascontiguousarray(
        _hadamard_bf16().reshape(8, 128, 8, 128).transpose(2, 1, 0, 3)
        .reshape(1024, 1024))

    nc = _get_nc()
    in_maps = []
    for core in range(8):
        b, qh = core // 2, core % 2
        xt = np.ascontiguousarray(np.roll(x[b].T, -qh * QL, axis=1))
        in_maps.append({
            "xT": xt, "wq": wq, "wk": wk, "wv": wv, "wo": wo,
            "boc": boc, "wfT": wfT, "h16d": h16,
        })
    res = run_bass_kernel_spmd(nc, in_maps, core_ids=list(range(8)))
    out = np.empty((4, S, D), dtype=np.float32)
    for core in range(8):
        b, qh = core // 2, core % 2
        out[b, qh * QL:(qh + 1) * QL, :] = res.results[core]["outT"].T
    return out


# version 20
# speedup vs baseline: 1.0460x; 1.0086x over previous
import numpy as np
import ml_dtypes
import concourse.bass as bass
import concourse.mybir as mybir
from concourse.bass_utils import run_bass_kernel_spmd
from concourse.bass_isa import ReduceOp
from concourse.library_config import attn as attn_lib
from concourse.library_overlay import lower_extended_insts

F32 = mybir.dt.float32
BF16 = mybir.dt.bfloat16
I32 = mybir.dt.int32
AF = mybir.ActivationFunctionType
OP = mybir.AluOpType
AX = mybir.AxisListType

D = 1024
S = 4096
QL = 2048
NB = 8
NG = 8
NQC = 4
NKC = 32
NKG = 16
MAGIC_EPS = 1e-5

ENGINES = ("sync", "vector", "scalar", "tensor", "gpsimd")


class Sems:

    NAMES = ("x", "xg", "xc", "xq", "fin", "sd", "rs", "bc", "xn", "wd", "wg", "wc",
             "ps", "ev", "q16", "mxv", "mxg", "acst", "xrel",
             "qk", "exp", "pvp", "rcp", "rcb", "a16", "mlt", "asp",
             "arel", "accr", "out")

    def __init__(self, nc):
        self.dummies = [nc.alloc_semaphore(f"s_dummy{i}") for i in range(24)]
        self.h = {n: nc.alloc_semaphore("s_" + n) for n in self.NAMES}
        self.reset()

    def reset(self):
        self.n = {k: 0 for k in self.NAMES}


class Em:

    def __init__(self, engine_name, eng, sems):
        self.who = engine_name
        self.eng = eng
        self.s = sems

    def op(self, who, fn, sem=None, k=1):
        if who == self.who:
            ins = fn(self.eng)
            if sem is not None:
                ins.then_inc(self.s.h[sem], k)
        if sem is not None:
            self.s.n[sem] += k

    def wait(self, who, sem, val=None):
        if who == self.who:
            v = self.s.n[sem] if val is None else val
            if v > 0:
                self.eng.wait_ge(self.s.h[sem], v)

    def cnt(self, sem):
        return self.s.n[sem]


def build_core_program(debug=False):
    nc = bass.Bass("TRN2", target_bir_lowering=False, detect_race_conditions=False)
    dbg_kind = "ExternalOutput" if debug else "Internal"

    xT = nc.dram_tensor("xT", [D, S], F32, kind="ExternalInput")
    wq = nc.dram_tensor("wq", [D, D], F32, kind="ExternalInput")
    wk = nc.dram_tensor("wk", [D, D], F32, kind="ExternalInput")
    wv = nc.dram_tensor("wv", [D, D], F32, kind="ExternalInput")
    wo = nc.dram_tensor("wo", [D, D], F32, kind="ExternalInput")
    boc = nc.dram_tensor("boc", [128, 8], F32, kind="ExternalInput")
    wfT = nc.dram_tensor("wfT", [D, D], F32, kind="ExternalInput")
    h16d = nc.dram_tensor("h16d", [D, D], BF16, kind="ExternalInput")
    outT = nc.dram_tensor("outT", [D, QL], F32, kind="ExternalOutput")

    acc_sp = nc.dram_tensor("acc_sp", [D, QL], F32, kind=dbg_kind)
    a_sp = nc.dram_tensor("a_sp", [D, QL], BF16, kind=dbg_kind)
    xn_dbg = nc.dram_tensor("xn_dbg", [D, S], BF16, kind=dbg_kind) if debug else None
    if debug:
        kT_dbg = nc.dram_tensor("kT_dbg", [128, S], BF16, kind=dbg_kind)
        qT_dbg = nc.dram_tensor("qT_dbg", [128, QL], BF16, kind=dbg_kind)
        vg_dbg = nc.dram_tensor("vg_dbg", [128, NKC * 130], BF16, kind=dbg_kind)
        rr_dbg = nc.dram_tensor("rr_dbg", [128, 512], F32, kind=dbg_kind)
        p16_dbg = nc.dram_tensor("p16_dbg", [128, 1024], BF16, kind=dbg_kind)
        xh_dbg = nc.dram_tensor("xh_dbg", [D, 512], F32, kind=dbg_kind)
        hb_dbg = nc.dram_tensor("hb_dbg", [128, 2048], BF16, kind=dbg_kind)
        mx_dbg = nc.dram_tensor("mx_dbg", [128, 2048], F32, kind=dbg_kind)
        q_dbg = nc.dram_tensor("q_dbg", [D, 512], BF16, kind=dbg_kind)

    xn16 = [nc.alloc_sbuf_tensor(f"xn16_{c}", [128, S], BF16) for c in range(8)]
    kT2 = [nc.alloc_sbuf_tensor(f"kT{i}", [128, S], BF16) for i in range(2)]
    vg = nc.alloc_sbuf_tensor("vg", [128, NKC * 130], BF16)
    qT = nc.alloc_sbuf_tensor("qT", [128, QL], BF16)
    p16 = [[nc.alloc_sbuf_tensor(f"p16_{h}_{b}", [128, 1024], BF16)
            for b in range(2)] for h in range(2)]
    asta1 = [nc.alloc_sbuf_tensor(f"asta_{h}", [64, 512], BF16) for h in range(2)]
    rrowT = nc.alloc_sbuf_tensor("rrowT", [128, 512], F32)
    rrow0 = nc.alloc_sbuf_tensor("rrow0", [1, 512], F32)
    rcb = [nc.alloc_sbuf_tensor(f"rcb{i}", [64, 512], F32) for i in range(2)]
    wst = [nc.alloc_sbuf_tensor(f"wst{i}", [128, 1024], F32) for i in range(2)]
    w16 = {m: nc.alloc_sbuf_tensor(f"w16{m}", [128, 1024], BF16) for m in "qkv"}
    ones16 = nc.alloc_sbuf_tensor("ones16", [128, 128], BF16)
    bocs = nc.alloc_sbuf_tensor("bocs", [128, 8], F32)
    xstage = [nc.alloc_sbuf_tensor(f"xstage{c}", [128, 512], F32) for c in range(8)]

    REGION = 66 * 1024
    arena = nc.alloc_sbuf_tensor("arena", [128, REGION], mybir.dt.uint8)
    arena_addr = nc.lookup_mloc(arena).addr

    def overlay(name, shape, dtype, off):
        per_part = int(np.prod(shape[1:])) * mybir.dt.size(dtype)
        assert off + per_part <= REGION, (name, off, per_part)
        h = nc.virtual_sbuf_tensor(name, shape, dtype)
        nc._tensor(h.name, list(shape), dtype, type="SB")
        ml = nc.lookup_mloc(h)
        ml.allocated = not nc.target_bir_lowering
        ml.addr = arena_addr + off
        return h

    o = 0
    tern16 = []
    for c in range(8):
        tern16.append(overlay(f"tern16_{c}", [128, D], BF16, o)); o += 2048
    xh32 = []
    for j in range(8):
        xh32.append(overlay(f"xh32_{j}", [128, 512], F32, o)); o += 2048
    q16t = []
    for j in range(8):
        q16t.append(overlay(f"q16t_{j}", [128, 512], BF16, 16384 + j * 2048))
    hbufB = []
    for p in range(8):
        hbufB.append(overlay(f"hbufB{p}", [128, 1024], BF16, o)); o += 2048
    O_MX = o
    mx1 = overlay("mx1", [128, 512], F32, o); o += 2048
    mx2 = overlay("mx2", [128, 512], F32, o); o += 2048
    r7B = overlay("r7B", [128, 512], F32, o); o += 2048
    srB = overlay("srB", [128, 512], F32, o); o += 2048
    wmB = overlay("wmB", [128, D], F32, o)
    fi32 = overlay("fi32", [128, 512], I32, o); o += 2048
    ff1 = overlay("ff1", [128, 512], F32, o); o += 2048
    ff2 = overlay("ff2", [128, 512], F32, o); o += 2048
    r_wm = overlay("r_wm", [1, D], F32, 16384 + 4 * 2048)
    r_rw = overlay("r_rw", [1, D], F32, 16384 + 6 * 2048)
    rwB = overlay("rwB", [128, D], F32, O_MX)
    F_END = o
    assert F_END <= REGION

    o = 0
    xb16 = []
    for c in range(8):
        xb16.append(overlay(f"xb16_{c}", [128, 512], BF16, o)); o += 1024
    xsq16 = []
    for c in range(8):
        xsq16.append(overlay(f"xsq16_{c}", [128, 512], BF16, o)); o += 1024
    xodd = []
    for c in range(8):
        xodd.append(overlay(f"xodd_{c}", [128, 512], F32, o)); o += 2048
    r_mean = [overlay("r_mean0", [1, 512], F32, o), overlay("r_mean1", [1, 512], F32, o + 2048)]
    o += 4096
    r_ms = [overlay("r_ms0", [1, 512], F32, o), overlay("r_ms1", [1, 512], F32, o + 2048)]
    o += 4096
    r_c = [overlay("r_c0", [1, 512], F32, o), overlay("r_c1", [1, 512], F32, o + 2048)]
    o += 4096
    meanB = [overlay("meanB0", [128, 512], F32, o), overlay("meanB1", [128, 512], F32, o + 2048)]
    o += 4096
    rsB = [overlay("rsB0", [128, 512], F32, o), overlay("rsB1", [128, 512], F32, o + 2048)]
    o += 4096
    tA = [overlay("tA", [128, 512], F32, o)] * 2
    o += 2048
    assert o <= REGION

    o = 0
    wo16 = []
    for c in range(8):
        wo16.append(overlay(f"wo16_{c}", [128, D], BF16, o)); o += 2048
    arel = []
    for c in range(8):
        arel.append(overlay(f"arel{c}", [128, 512], BF16, o)); o += 1024
    outst = [overlay("outst0", [128, 512], F32, o), overlay("outst1", [128, 512], F32, o + 2048)]
    o += 4096
    D_END = o
    p16x = [overlay(f"p16x{h}", [128, 1024], BF16, D_END + h * 2048) for h in range(2)]
    av65 = [overlay(f"av65_{h}", [65, 512], F32, D_END + 4096 + h * 2048) for h in range(2)]
    qT2b = overlay("qT2b", [128, 2048], BF16, D_END + 8192)
    assert o <= REGION

    fost = [mx1, mx2]

    pbS = [nc.alloc_psum_tensor(f"pbS{i}", [128, 1024], F32) for i in range(2)]
    pbA2 = [[nc.alloc_psum_tensor(f"pbA{h}_{p}", [128, 512], F32) for p in range(2)]
            for h in range(2)]
    pbT = [pbS[0][:, 0:512], pbS[1][:, 0:512]]

    for h in range(2):
        p16[h].append(p16x[h])
    qT2 = [qT, qT2b]
    sems = Sems(nc)

    def walk(em: Em):
        sems.reset()
        st = {"tile": 0, "ev_snap": [],
              "sbank_free": [0, 0],
              "pbuf_free": [[0, 0, 0], [0, 0, 0]],
              "pvpsum_free": [0, 0],
              "w16_free": 0,
              "rcb_free": [0, 0],
              "av_free": [0, 0],
              "rrow_free": [0, 0],
              "asta_free": 0,
              "outst_free": [0, 0],
              }

        def acc_tile(n_mm, mk_mm, evac, banks=None, te_pre=None):
            k = st["tile"]
            st["tile"] += 1
            bank = (banks or pbT)[k % 2]
            if k >= 2:
                em.wait("tensor", "ev", st["ev_snap"][k - 2] + 1)
            if te_pre is not None:
                te_pre()
            for i in range(n_mm):
                em.op("tensor", lambda t, i=i, bank=bank: mk_mm(t, i, bank),
                      sem="ps" if i == n_mm - 1 else None)
            em.wait("vector", "ps", em.cnt("ps"))
            st["ev_snap"].append(em.cnt("ev"))
            evac(bank)

        em.op("gpsimd", lambda g: g.load_library(attn_lib))

        em.op("vector", lambda v: v.memset(ones16[:], 1.0))
        for pb in (pbS[0], pbS[1], pbA2[0][0], pbA2[0][1], pbA2[1][0], pbA2[1][1]):
            em.op("vector", lambda v, pb=pb: v.memset(pb[:], 0.0))
        if debug:
            em.op("vector", lambda v: v.memset(hbufB[0][:], 777.0))
            em.op("vector", lambda v: v.memset(hbufB[1][:], 777.0))
        vgr = vg[:].rearrange("p (t s c) -> p t s c", t=NKC, s=2, c=65)
        em.op("vector", lambda v: v.memset(vgr[:, :, :, 64:65], 1.0))
        em.op("sync", lambda sy: sy.dma_start(out=bocs[:], in_=boc[:]), sem="wd", k=16)

        for tb in range(NB):
            par = tb % 2
            xbl = xstage if par == 0 else xodd
            if tb >= 2:
                em.wait("sync", "xn", tb - 1)
                em.wait("gpsimd", "xn", tb - 1)
            em.wait("sync", "x", em.cnt("x"))
            em.wait("gpsimd", "xg", em.cnt("xg"))
            for c in range(8):
                who = "sync" if c < 4 else "gpsimd"
                em.op(who, lambda e, c=c: e.dma_start(
                    out=xbl[c][:], in_=xT[c * 128:(c + 1) * 128, tb * 512:(tb + 1) * 512]),
                    sem="x" if c < 4 else "xg", k=16)
            em.wait("vector", "x", em.cnt("x"))
            em.wait("vector", "xg", em.cnt("xg"))
            em.wait("scalar", "x", em.cnt("x"))
            em.wait("scalar", "xg", em.cnt("xg"))
            for c in range(8):
                em.op("vector", lambda v, c=c: v.tensor_copy(out=xb16[c][:], in_=xbl[c][:]),
                      sem="xc" if c == 7 else None)
                em.op("scalar", lambda sc, c=c: sc.activation(
                    out=xsq16[c][:], in_=xbl[c][:], func=AF.Square),
                    sem="xq" if c == 7 else None)

            def mk_sum(t, i, bank, src):
                return t.matmul(bank[:], ones16[:], src[i][:],
                                start=(i == 0), stop=(i == 7))

            def ev_sum(bank, dst):
                em.op("vector", lambda v: v.tensor_scalar(
                    out=dst[:], in0=bank[0:1, :],
                    scalar1=1.0 / 1024.0, scalar2=None, op0=OP.mult), sem="ev")

            em.wait("tensor", "xc", em.cnt("xc"))
            em.wait("tensor", "xq", em.cnt("xq"))
            acc_tile(8, lambda t, i, bank: mk_sum(t, i, bank, xb16),
                     lambda bank: ev_sum(bank, r_mean[par]))
            acc_tile(8, lambda t, i, bank: mk_sum(t, i, bank, xsq16),
                     lambda bank: ev_sum(bank, r_ms[par]))
            em.wait("vector", "ev", em.cnt("ev"))
            em.op("vector", lambda v: v.tensor_mul(
                r_c[par][:], r_mean[par][:], r_mean[par][:]))
            em.op("vector", lambda v: v.tensor_sub(
                r_ms[par][:], r_ms[par][:], r_c[par][:]))
            em.op("vector", lambda v: v.tensor_scalar_add(
                r_ms[par][:], r_ms[par][:], MAGIC_EPS), sem="fin")
            em.wait("scalar", "fin", em.cnt("fin"))
            em.op("scalar", lambda sc: sc.activation(
                out=r_c[par][:], in_=r_ms[par][:], func=AF.Sqrt), sem="sd")
            em.wait("vector", "sd", em.cnt("sd"))
            em.op("vector", lambda v: v.reciprocal(
                out=r_c[par][:], in_=r_c[par][:]), sem="rs")
            em.wait("gpsimd", "rs", em.cnt("rs"))
            if tb >= 2:
                em.wait("gpsimd", "xn", tb - 1)
            em.op("gpsimd", lambda g: g.partition_broadcast(
                meanB[par][:], r_mean[par][:], 128))
            em.op("gpsimd", lambda g: g.partition_broadcast(
                rsB[par][:], r_c[par][:], 128), sem="bc")
            em.wait("vector", "bc", em.cnt("bc"))
            for c in range(8):
                em.op("vector", lambda v, c=c: v.tensor_sub(tA[par][:], xbl[c][:], meanB[par][:]))
                em.op("vector", lambda v, c=c: v.tensor_mul(
                    xn16[c][:, tb * 512:(tb + 1) * 512], tA[par][:], rsB[par][:]),
                    sem="xn" if c == 7 else None)

        if debug:
            em.wait("gpsimd", "xn", NB)
            for c in range(8):
                em.op("gpsimd", lambda gp, c=c: gp.dma_start(
                    out=xn_dbg[c * 128:(c + 1) * 128, :], in_=xn16[c][:]), sem="out", k=16)

        for c in range(8):
            em.wait("sync", "wc", em.cnt("wc") - 1)
            em.op("sync", lambda sy, c=c: sy.dma_start(
                out=wst[c % 2][:], in_=wfT[c * 128:(c + 1) * 128, :]), sem="wd", k=16)
            em.wait("scalar", "wd", em.cnt("wd"))
            em.op("scalar", lambda sc, c=c: sc.activation(
                out=tern16[c][:], in_=wst[c % 2][:], func=AF.Abs), sem="wc")

        def mk_wsum(t, i, bank, half):
            return t.matmul(bank[:], ones16[:],
                            tern16[i][:, half * 512:(half + 1) * 512],
                            start=(i == 0), stop=(i == 7))

        for half in range(2):
            def ev_wsum(bank, half=half):
                em.op("vector", lambda v: v.tensor_scalar(
                    out=r_wm[0:1, half * 512:(half + 1) * 512], in0=bank[0:1, :],
                    scalar1=1.0 / 1024.0, scalar2=None, op0=OP.mult), sem="ev")
            em.wait("tensor", "wc", em.cnt("wc"))
            acc_tile(8, lambda t, i, bank, half=half: mk_wsum(t, i, bank, half), ev_wsum)
        em.op("vector", lambda v: v.reciprocal(out=r_rw[:], in_=r_wm[:]), sem="rs")
        em.wait("gpsimd", "rs", em.cnt("rs"))
        em.op("gpsimd", lambda g: g.partition_broadcast(rwB[:], r_rw[:], 128))
        em.op("gpsimd", lambda g: g.partition_broadcast(wmB[:], r_wm[:], 128), sem="bc")

        for c in range(8):
            em.wait("sync", "wc", em.cnt("wc") - 1)
            em.op("sync", lambda sy, c=c: sy.dma_start(
                out=wst[c % 2][:], in_=wfT[c * 128:(c + 1) * 128, :]), sem="wd", k=16)
            em.wait("vector", "wd", em.cnt("wd"))
            em.wait("vector", "bc", em.cnt("bc"))
            em.op("vector", lambda v, c=c: v.tensor_mul(xh32[0][:], wst[c % 2][:, 0:512], rwB[:, 0:512]))
            em.op("vector", lambda v, c=c: v.tensor_mul(xh32[1][:], wst[c % 2][:, 512:1024], rwB[:, 512:1024]))
            for half in range(2):
                em.op("vector", lambda v, c=c, half=half: v.tensor_scalar(
                    out=xh32[2][:], in0=xh32[half][:], scalar1=0.5, scalar2=None, op0=OP.is_gt))
                em.op("vector", lambda v, c=c, half=half: v.scalar_tensor_tensor(
                    out=xh32[2][:], in0=xh32[half][:], scalar=-0.5, in1=xh32[2][:],
                    op0=OP.is_ge, op1=OP.add))
                em.op("vector", lambda v, c=c, half=half: v.tensor_scalar(
                    out=xh32[2][:], in0=xh32[2][:],
                    scalar1=1.0, scalar2=None, op0=OP.subtract))
                em.op("vector", lambda v, c=c, half=half: v.tensor_mul(
                    tern16[c][:, half * 512:(half + 1) * 512], xh32[2][:],
                    wmB[:, half * 512:(half + 1) * 512]),
                    sem="wc" if half == 1 else None)

        em.wait("sync", "xn", NB)
        for j in range(8):
            em.op("sync", lambda sy, j=j: sy.dma_start(
                out=hbufB[j][:], in_=h16d[j * 128:(j + 1) * 128, :]),
                sem="wg", k=16)
        for tc in range(NQC):
            if tc >= 1:
                em.wait("sync", "acst", em.cnt("acst"))
            for c in range(8):
                em.op("sync", lambda sy, c=c, tc=tc: sy.dma_start(
                    out=xstage[c][:], in_=xT[c * 128:(c + 1) * 128, tc * 512:(tc + 1) * 512]),
                    sem="xrel", k=16)
            if tc == 0:
                em.wait("tensor", "wg", em.cnt("wg"))
            for j in range(8):
                def mk_xh(t, i, bank, j=j, tc=tc):
                    return t.matmul(bank[:], hbufB[j][:, i * 128:(i + 1) * 128],
                                    xn16[i][:, tc * 512:(tc + 1) * 512],
                                    start=(i == 0), stop=(i == 7))

                def ev_xh(bank, j=j):
                    em.op("vector", lambda v: v.tensor_copy(out=xh32[j][:], in_=bank[:]))
                    if j == 0:
                        em.wait("vector", "acst", em.cnt("acst"))
                        em.op("vector", lambda v: v.tensor_copy(out=mx1[:], in_=xh32[j][:]))
                        em.op("vector", lambda v: v.tensor_copy(out=mx2[:], in_=xh32[j][:]), sem="ev")
                    else:
                        em.op("vector", lambda v: v.tensor_max(mx1[:], mx1[:], xh32[j][:]))
                        em.op("vector", lambda v: v.tensor_tensor(mx2[:], mx2[:], xh32[j][:], OP.min),
                              sem="ev")
                    if j == 7:
                        em.op("vector", lambda v: v.tensor_scalar_mul(mx2[:], mx2[:], -1.0))
                        em.op("vector", lambda v: v.tensor_max(mx1[:], mx1[:], mx2[:]), sem="mxv")

                acc_tile(8, mk_xh, ev_xh, banks=[pbA2[0][0][:], pbA2[0][1][:]])

            em.wait("gpsimd", "mxv", em.cnt("mxv"))
            if debug and tc == 0:
                em.op("gpsimd", lambda g: g.dma_start(
                    out=mx_dbg[:, 1536:2048], in_=mx1[:]), sem="asp", k=16)
            em.op("gpsimd", lambda g: g.partition_all_reduce(srB[:], mx1[:], 128, ReduceOp.max),
                  sem="mxg")
            em.wait("vector", "mxg", em.cnt("mxg"))
            em.op("vector", lambda v: v.tensor_scalar_max(mx2[:], srB[:], MAGIC_EPS))
            em.op("vector", lambda v: v.reciprocal(out=r7B[:], in_=mx2[:]))
            em.op("vector", lambda v: v.tensor_scalar_mul(r7B[:], r7B[:], 7.0))
            em.op("vector", lambda v: v.tensor_scalar_mul(srB[:], mx2[:], 1.0 / 7.0))
            for j in range(8):
                em.op("vector", lambda v, j=j: v.tensor_mul(ff1[:], xh32[j][:], r7B[:]))
                em.op("vector", lambda v: v.tensor_copy(out=fi32[:], in_=ff1[:]))
                em.op("vector", lambda v: v.tensor_copy(out=ff2[:], in_=fi32[:]))
                em.op("vector", lambda v: v.tensor_sub(ff1[:], ff1[:], ff2[:]))
                em.op("vector", lambda v: v.scalar_tensor_tensor(
                    out=ff2[:], in0=ff1[:], scalar=0.5, in1=ff2[:], op0=OP.is_gt, op1=OP.add))
                em.op("vector", lambda v: v.scalar_tensor_tensor(
                    out=ff2[:], in0=ff1[:], scalar=-0.5, in1=ff2[:], op0=OP.is_ge, op1=OP.add))
                em.op("vector", lambda v: v.tensor_scalar(
                    out=ff2[:], in0=ff2[:], scalar1=1.0, scalar2=7.0, op0=OP.subtract, op1=OP.min))
                em.op("vector", lambda v, j=j: v.tensor_scalar_max(q16t[j][:], ff2[:], -8.0),
                      sem="q16" if j == 7 else None)

            if debug and tc == 0:
                em.wait("gpsimd", "q16", em.cnt("q16"))
                for j in range(8):
                    em.op("gpsimd", lambda gp, j=j: gp.dma_start(
                        out=xh_dbg[j * 128:(j + 1) * 128, :], in_=xh32[j][:]), sem="asp", k=16)
                    em.op("gpsimd", lambda gp, j=j: gp.dma_start(
                        out=q_dbg[j * 128:(j + 1) * 128, :], in_=q16t[j][:]), sem="asp", k=16)
                em.op("gpsimd", lambda gp: gp.dma_start(out=hb_dbg[:, 0:1024], in_=hbufB[0][:]), sem="asp", k=16)
                em.op("gpsimd", lambda gp: gp.dma_start(out=hb_dbg[:, 1024:2048], in_=hbufB[1][:]), sem="asp", k=16)
                em.op("gpsimd", lambda gp: gp.dma_start(out=mx_dbg[:, 0:512], in_=mx1[:]), sem="asp", k=16)
                em.op("gpsimd", lambda gp: gp.dma_start(out=mx_dbg[:, 512:1024], in_=r7B[:]), sem="asp", k=16)
                em.op("gpsimd", lambda gp: gp.dma_start(out=mx_dbg[:, 1024:1536], in_=srB[:]), sem="asp", k=16)
                em.wait("tensor", "asp", em.cnt("asp"))

            for o in range(8):
                def mk_ffn(t, i, bank, o=o):
                    return t.matmul(bank[:], tern16[i][:, o * 128:(o + 1) * 128],
                                    q16t[i][:], start=(i == 0), stop=(i == 7))

                def ev_ffn(bank, o=o, tc=tc):
                    par = o % 2
                    em.wait("vector", "acst", st["outst_free"][par])
                    em.op("vector", lambda v: v.tensor_mul(ff1[:], bank[:], srB[:]))
                    em.wait("vector", "xrel", em.cnt("xrel"))
                    em.op("vector", lambda v: v.tensor_scalar_add(ff2[:], xstage[o][:], bocs[:, o:o + 1]))
                    em.op("vector", lambda v: v.tensor_add(fost[par][:], ff1[:], ff2[:]), sem="ev")
                    em.wait("gpsimd", "ev", em.cnt("ev"))
                    em.op("gpsimd", lambda g: g.dma_start(
                        out=acc_sp[o * 128:(o + 1) * 128, tc * 512:(tc + 1) * 512],
                        in_=fost[par][:]), sem="acst", k=16)
                    st["outst_free"][par] = em.cnt("acst")

                em.wait("tensor", "q16", em.cnt("q16"))
                em.wait("tensor", "wc", em.cnt("wc"))
                acc_tile(8, mk_ffn, ev_ffn, banks=[pbA2[1][0][:], pbA2[1][1][:]])

        def load_w_group(g):
            for mi, (mat, wdr) in enumerate((("q", wq), ("k", wk), ("v", wv))):
                em.wait("sync", "wc", em.cnt("wc"))
                em.op("sync", lambda sy, wdr=wdr, g=g, mi=mi: sy.dma_start(
                    out=wst[mi % 2][:], in_=wdr[g * 128:(g + 1) * 128, :]),
                    sem="wd", k=16)
                em.wait("vector", "wd", em.cnt("wd"))
                em.wait("vector", "ps", st["w16_free"])
                em.op("vector", lambda v, mat=mat, mi=mi: v.tensor_copy(
                    out=w16[mat][:], in_=wst[mi % 2][:]), sem="wc")

        def emit_k_tiles(g, banks, te_pre_first):
            for t8 in range(8):
                def mk_k(t, i, bank, t8=t8, g=g):
                    return t.matmul(bank[:], w16["k"][:, i * 128:(i + 1) * 128],
                                    xn16[i][:, t8 * 512:(t8 + 1) * 512],
                                    start=(i == 0), stop=(i == 7))

                def ev_k(bank, t8=t8, g=g):
                    em.op("vector", lambda v: v.tensor_copy(
                        out=kT2[g % 2][:, t8 * 512:(t8 + 1) * 512], in_=bank[:]), sem="ev")
                acc_tile(8, mk_k, ev_k, banks=banks,
                         te_pre=te_pre_first if t8 == 0 else None)

        load_w_group(0)
        for g in range(NG):
            em.wait("tensor", "xn", NB)
            em.wait("tensor", "wc", em.cnt("wc"))
            em.wait("tensor", "exp", em.cnt("exp"))
            if g == 0:
                for tq in range(NQC):
                    def mk_q(t, i, bank, tq=tq):
                        return t.matmul(bank[:], w16["q"][:, i * 128:(i + 1) * 128],
                                        xn16[i][:, tq * 512:(tq + 1) * 512],
                                        start=(i == 0), stop=(i == 7))

                    def ev_q(bank, tq=tq):
                        em.op("vector", lambda v: v.tensor_scalar(
                            out=qT2[0][:, tq * 512:(tq + 1) * 512], in0=bank[:],
                            scalar1=0.125, scalar2=None, op0=OP.mult), sem="ev")
                    acc_tile(8, mk_q, ev_q)
            if g == 0:
                emit_k_tiles(0, pbT, None)
            def emit_v_tile(tt):
                def mk_v(t, i, bank, tt=tt):
                    return t.matmul(bank[:, 0:128], xn16[i][:, tt * 128:(tt + 1) * 128],
                                    w16["v"][:, i * 128:(i + 1) * 128],
                                    start=(i == 0), stop=(i == 7))

                def ev_v(bank, tt=tt):
                    em.op("vector", lambda v: v.tensor_copy(
                        out=vgr[:, tt, :, 0:64],
                        in_=bank[:, 0:128].rearrange("p (s c) -> p s c", s=2)), sem="ev")

                def pre_v(tt=tt):
                    if tt < 2:
                        em.wait("tensor", "a16", st["pvpsum_free"][1])
                acc_tile(8, mk_v, ev_v, banks=[pbA2[0][1][:], pbA2[1][1][:]], te_pre=pre_v)
                st["v_ready"][tt] = em.cnt("ev")

            st["v_ready"] = [0] * NKC
            st.setdefault("k_ev_snap", 0)

            for qc in range(NQC):
                apar = qc % 2
                exp_done = [[0] * NKG for _ in range(2)]
                for kg in range(NKG + 2):
                    if qc == 0 and kg == 0:
                        st["qk_ev_snap"] = em.cnt("ev")
                    if kg < NKG:
                        for hh in range(2):
                            em.wait("tensor", "exp", st["sbank_free"][hh])
                            if qc == 0 and kg == 0 and hh == 0:
                                em.wait("tensor", "ev", st["qk_ev_snap"])
                            for half in range(2):
                                kc = 2 * kg + half
                                em.op("tensor", lambda t, hh=hh, half=half, kc=kc, qc=qc: t.matmul(
                                    pbS[hh][:, half * 512:(half + 1) * 512],
                                    kT2[g % 2][hh * 64:(hh + 1) * 64, kc * 128:(kc + 1) * 128],
                                    qT2[g % 2][hh * 64:(hh + 1) * 64, qc * 512:(qc + 1) * 512],
                                    start=True, stop=True),
                                    sem="qk" if half == 1 else None)
                            em.wait("scalar", "qk", em.cnt("qk"))
                            em.wait("scalar", "pvp", st["pbuf_free"][hh][kg % 3])
                            em.op("scalar", lambda sc, hh=hh, kg=kg: sc.activation(
                                out=p16[hh][kg % 3][:], in_=pbS[hh][:], func=AF.Exp), sem="exp")
                            st["sbank_free"][hh] = em.cnt("exp")
                            exp_done[hh][kg] = em.cnt("exp")
                    if kg >= 2:
                        pk = kg - 2
                        for hh in range(2):
                            em.wait("tensor", "exp", exp_done[hh][pk])
                            if qc == 0 and hh == 0:
                                em.wait("tensor", "ev", st["v_ready"][2 * pk + 1])
                            if pk == 0:
                                em.wait("tensor", "a16", st["pvpsum_free"][apar])
                                if qc == 1 and hh == 0:
                                    em.wait("tensor", "ev", st["v_ready"][NKC - 1])
                                if qc == 3 and hh == 0 and st["k_ev_snap"] > 0:
                                    em.wait("tensor", "ev", st["k_ev_snap"])
                            for half in range(2):
                                kc = 2 * pk + half
                                em.op("tensor", lambda t, hh=hh, half=half, kc=kc, pk=pk, apar=apar: t.matmul(
                                    pbA2[hh][apar][0:65, :],
                                    vgr[:, kc, hh, :],
                                    p16[hh][pk % 3][:, half * 512:(half + 1) * 512],
                                    start=(kc == 0), stop=(kc == NKC - 1)),
                                    sem="pvp" if half == 1 else None)
                            st["pbuf_free"][hh][pk % 3] = em.cnt("pvp")
                    if qc == 2 and kg % 2 == 0 and kg < NKG and g + 1 < NG:
                        t8 = kg // 2

                        def mk_k2(t, i, bank, t8=t8, gg=g + 1):
                            return t.matmul(bank[:], w16["k"][:, i * 128:(i + 1) * 128],
                                            xn16[i][:, t8 * 512:(t8 + 1) * 512],
                                            start=(i == 0), stop=(i == 7))

                        def ev_k2(bank, t8=t8, gg=g + 1):
                            em.op("vector", lambda v: v.tensor_copy(
                                out=kT2[gg % 2][:, t8 * 512:(t8 + 1) * 512],
                                in_=bank[:]), sem="ev")

                        def pre_k2(t8=t8):
                            if t8 == 0:
                                em.wait("tensor", "a16", st["pvpsum_free"][1])
                        acc_tile(8, mk_k2, ev_k2,
                                 banks=[pbA2[0][1][:], pbA2[1][1][:]], te_pre=pre_k2)
                        st["k_ev_snap"] = em.cnt("ev")
                    if qc == 3 and kg % 2 == 1 and kg < 8 and g + 1 < NG:
                        tq = kg // 2

                        def mk_q2(t, i, bank, tq=tq):
                            return t.matmul(bank[:], w16["q"][:, i * 128:(i + 1) * 128],
                                            xn16[i][:, tq * 512:(tq + 1) * 512],
                                            start=(i == 0), stop=(i == 7))

                        def ev_q2(bank, tq=tq, gg=g + 1):
                            em.op("vector", lambda v: v.tensor_scalar(
                                out=qT2[gg % 2][:, tq * 512:(tq + 1) * 512], in0=bank[:],
                                scalar1=0.125, scalar2=None, op0=OP.mult), sem="ev")

                        def pre_q2(tq=tq):
                            if tq == 0:
                                em.wait("tensor", "a16", st["pvpsum_free"][0])
                                em.wait("tensor", "wc", em.cnt("wc"))
                        acc_tile(8, mk_q2, ev_q2,
                                 banks=[pbA2[0][0][:], pbA2[1][0][:]], te_pre=pre_q2)
                    if qc == 0 and kg < NKG:
                        for tt in (2 * kg, 2 * kg + 1):
                            emit_v_tile(tt)
                        if kg == NKG - 1:
                            st["w16_free"] = em.cnt("ps")
                            if g + 1 < NG:
                                load_w_group(g + 1)

                em.wait("vector", "pvp", em.cnt("pvp"))
                for hh in range(2):
                    em.wait("vector", "mlt", st["av_free"][hh])
                    em.op("vector", lambda v, hh=hh, apar=apar: v.tensor_copy(
                        out=av65[hh][:], in_=pbA2[hh][apar][0:65, :]), sem="a16")
                st["pvpsum_free"][apar] = em.cnt("a16")
                st["av_free"] = [em.cnt("mlt") + 1, em.cnt("mlt") + 2]
                for hh in range(2):
                    em.wait("vector", "rcb", st["rrow_free"][hh])
                    em.op("vector", lambda v, hh=hh: v.reciprocal(
                        out=av65[hh][64:65, :], in_=av65[hh][64:65, :]))
                    em.op("vector", lambda v, hh=hh: v.tensor_copy(
                        out=rrow0[:], in_=av65[hh][64:65, :]), sem="rcp")
                    em.wait("gpsimd", "rcp", em.cnt("rcp"))
                    em.wait("gpsimd", "mlt", st["rcb_free"][hh])
                    em.op("gpsimd", lambda gp, hh=hh: gp.partition_broadcast(
                        rcb[hh][:], rrow0[:], 64), sem="rcb")
                    st["rrow_free"][hh] = em.cnt("rcb")
                    st["rcb_free"][hh] = em.cnt("mlt") + 1 + hh
                em.wait("vector", "rcb", em.cnt("rcb"))
                em.wait("vector", "asp", st["asta_free"])
                for hh in range(2):
                    em.op("vector", lambda v, hh=hh: v.tensor_mul(
                        asta1[hh][:], av65[hh][0:64, :], rcb[hh][:]), sem="mlt")
                em.wait("gpsimd", "mlt", em.cnt("mlt"))
                for hh in range(2):
                    em.op("gpsimd", lambda gp, g=g, qc=qc, hh=hh: gp.dma_start(
                        out=a_sp[g * 128 + hh * 64:g * 128 + (hh + 1) * 64,
                                 qc * 512:(qc + 1) * 512],
                        in_=asta1[hh][:]), sem="asp", k=16)
                st["asta_free"] = em.cnt("asp")

        st["outst_free"] = [0, 0]
        for c in range(8):
            em.wait("sync", "wc", em.cnt("wc") - (0 if c < 2 else 1))
            em.op("sync", lambda sy, c=c: sy.dma_start(
                out=wst[c % 2][:], in_=wo[c * 128:(c + 1) * 128, :]), sem="wd", k=16)
            em.wait("vector", "wd", em.cnt("wd"))
            em.op("vector", lambda v, c=c: v.tensor_copy(out=wo16[c][:], in_=wst[c % 2][:]),
                  sem="wc")

        for tc in range(NQC):
            em.wait("sync", "asp", em.cnt("asp"))
            if tc >= 1:
                em.wait("sync", "ps", em.cnt("ps"))
                em.wait("sync", "ev", em.cnt("ev"))
            for hd in range(8):
                em.op("sync", lambda sy, hd=hd, tc=tc: sy.dma_start(
                    out=arel[hd][:], in_=a_sp[hd * 128:(hd + 1) * 128, tc * 512:(tc + 1) * 512]),
                    sem="arel", k=16)
            em.wait("sync", "acst", em.cnt("acst"))
            for o in range(8):
                em.op("sync", lambda sy, o=o, tc=tc: sy.dma_start(
                    out=xstage[o][:], in_=acc_sp[o * 128:(o + 1) * 128, tc * 512:(tc + 1) * 512]),
                    sem="accr", k=16)
            for o in range(8):
                def mk_wo(t, i, bank, o=o):
                    return t.matmul(bank[:], wo16[i][:, o * 128:(o + 1) * 128],
                                    arel[i][:], start=(i == 0), stop=(i == 7))

                def ev_wo(bank, o=o, tc=tc):
                    par = o % 2
                    em.wait("vector", "out", st["outst_free"][par])
                    em.wait("vector", "accr", em.cnt("accr"))
                    em.op("vector", lambda v: v.tensor_add(outst[par][:], xstage[o][:], bank[:]),
                          sem="ev")
                    em.wait("gpsimd", "ev", em.cnt("ev"))
                    em.op("gpsimd", lambda gp: gp.dma_start(
                        out=outT[o * 128:(o + 1) * 128, tc * 512:(tc + 1) * 512],
                        in_=outst[par][:]), sem="out", k=16)
                    st["outst_free"][par] = em.cnt("out")

                em.wait("tensor", "arel", em.cnt("arel"))
                acc_tile(8, mk_wo, ev_wo)

        em.wait("gpsimd", "out", em.cnt("out"))
        em.wait("sync", "out", em.cnt("out"))
        if debug:
            em.op("sync", lambda sy: sy.dma_start(out=kT_dbg[:], in_=kT2[(NG - 1) % 2][:]), sem="arel", k=16)
            em.op("sync", lambda sy: sy.dma_start(out=qT_dbg[:], in_=qT[:]), sem="arel", k=16)
            em.op("sync", lambda sy: sy.dma_start(out=vg_dbg[:], in_=vg[:]), sem="arel", k=16)
            em.op("sync", lambda sy: sy.dma_start(out=rr_dbg[:], in_=rrowT[:]), sem="arel", k=16)
            em.op("sync", lambda sy: sy.dma_start(out=p16_dbg[:], in_=p16[0][1][:]), sem="arel", k=16)
            em.wait("sync", "arel", em.cnt("arel"))


    with nc.Block() as blk:
        final_counts = {}

        def run(engine_name):
            def f(eng):
                em = Em(engine_name, eng, sems)
                walk(em)
                if final_counts:
                    assert final_counts == sems.n, (
                        "sem count divergence across walks", engine_name,
                        {k: (final_counts[k], sems.n[k]) for k in sems.n
                         if final_counts[k] != sems.n[k]})
                else:
                    final_counts.update(sems.n)
            return f

        blk.sync(run("sync"))
        blk.vector(run("vector"))
        blk.scalar(run("scalar"))
        blk.tensor(run("tensor"))
        blk.gpsimd(run("gpsimd"))

    lower_extended_insts(nc)
    return nc


_NC_CACHE = None


def _get_nc():
    global _NC_CACHE
    if _NC_CACHE is None:
        _NC_CACHE = build_core_program()
    return _NC_CACHE


def _tile_cols(w):
    return np.ascontiguousarray(
        w.reshape(8, 128, 8, 128).transpose(2, 1, 0, 3).reshape(1024, 1024))


def _hadamard_bf16():
    h = np.array([[1.0]], dtype=np.float64)
    while h.shape[0] < D:
        h = np.block([[h, h], [h, -h]])
    return (h / np.sqrt(D)).astype(np.float32).astype(ml_dtypes.bfloat16)


def kernel(L0_attn_x, L0_attn_Wq, L0_attn_Wk, L0_attn_Wv, L0_attn_Wo,
           L0_attn_bo, L0_ffn_weight):
    x = np.asarray(L0_attn_x, dtype=np.float32)
    wq = _tile_cols(np.asarray(L0_attn_Wq, dtype=np.float32))
    wk = _tile_cols(np.asarray(L0_attn_Wk, dtype=np.float32))
    wv = _tile_cols(np.asarray(L0_attn_Wv, dtype=np.float32))
    wo = np.ascontiguousarray(np.asarray(L0_attn_Wo, dtype=np.float32))
    bo = np.asarray(L0_attn_bo, dtype=np.float32)
    wf = np.asarray(L0_ffn_weight, dtype=np.float32)

    boc = np.ascontiguousarray(bo.reshape(8, 128).T)
    wfT = np.ascontiguousarray(wf.T)
    h16 = np.ascontiguousarray(
        _hadamard_bf16().reshape(8, 128, 8, 128).transpose(2, 1, 0, 3)
        .reshape(1024, 1024))

    nc = _get_nc()
    in_maps = []
    for core in range(8):
        b, qh = core // 2, core % 2
        xt = np.ascontiguousarray(np.roll(x[b].T, -qh * QL, axis=1))
        in_maps.append({
            "xT": xt, "wq": wq, "wk": wk, "wv": wv, "wo": wo,
            "boc": boc, "wfT": wfT, "h16d": h16,
        })
    res = run_bass_kernel_spmd(nc, in_maps, core_ids=list(range(8)))
    out = np.empty((4, S, D), dtype=np.float32)
    for core in range(8):
        b, qh = core // 2, core % 2
        out[b, qh * QL:(qh + 1) * QL, :] = res.results[core]["outT"].T
    return out


# version 21 (final)
# speedup vs baseline: 1.0483x; 1.0022x over previous
import numpy as np
import ml_dtypes
import concourse.bass as bass
import concourse.mybir as mybir
from concourse.bass_utils import run_bass_kernel_spmd
from concourse.bass_isa import ReduceOp
from concourse.library_config import attn as attn_lib
from concourse.library_overlay import lower_extended_insts

F32 = mybir.dt.float32
BF16 = mybir.dt.bfloat16
I32 = mybir.dt.int32
AF = mybir.ActivationFunctionType
OP = mybir.AluOpType
AX = mybir.AxisListType

D = 1024
S = 4096
QL = 2048
NB = 8
NG = 8
NQC = 4
NKC = 32
NKG = 16
MAGIC_EPS = 1e-5

ENGINES = ("sync", "vector", "scalar", "tensor", "gpsimd")


class Sems:

    NAMES = ("x", "xg", "xc", "xq", "fin", "sd", "rs", "bc", "xn", "wd", "wg", "wc",
             "ps", "ev", "q16", "mxv", "mxg", "acst", "xrel",
             "qk", "exp", "pvp", "rcp", "rcb", "a16", "mlt", "asp",
             "arel", "accr", "out")

    def __init__(self, nc):
        self.dummies = [nc.alloc_semaphore(f"s_dummy{i}") for i in range(24)]
        self.h = {n: nc.alloc_semaphore("s_" + n) for n in self.NAMES}
        self.reset()

    def reset(self):
        self.n = {k: 0 for k in self.NAMES}


class Em:

    def __init__(self, engine_name, eng, sems):
        self.who = engine_name
        self.eng = eng
        self.s = sems

    def op(self, who, fn, sem=None, k=1):
        if who == self.who:
            ins = fn(self.eng)
            if sem is not None:
                ins.then_inc(self.s.h[sem], k)
        if sem is not None:
            self.s.n[sem] += k

    def wait(self, who, sem, val=None):
        if who == self.who:
            v = self.s.n[sem] if val is None else val
            if v > 0:
                self.eng.wait_ge(self.s.h[sem], v)

    def cnt(self, sem):
        return self.s.n[sem]


def build_core_program(debug=False):
    nc = bass.Bass("TRN2", target_bir_lowering=False, detect_race_conditions=False)
    dbg_kind = "ExternalOutput" if debug else "Internal"

    xT = nc.dram_tensor("xT", [D, S], F32, kind="ExternalInput")
    wq = nc.dram_tensor("wq", [D, D], F32, kind="ExternalInput")
    wk = nc.dram_tensor("wk", [D, D], F32, kind="ExternalInput")
    wv = nc.dram_tensor("wv", [D, D], F32, kind="ExternalInput")
    wo = nc.dram_tensor("wo", [D, D], F32, kind="ExternalInput")
    boc = nc.dram_tensor("boc", [128, 8], F32, kind="ExternalInput")
    wfT = nc.dram_tensor("wfT", [D, D], F32, kind="ExternalInput")
    h16d = nc.dram_tensor("h16d", [D, D], BF16, kind="ExternalInput")
    outT = nc.dram_tensor("outT", [D, QL], F32, kind="ExternalOutput")

    acc_sp = nc.dram_tensor("acc_sp", [D, QL], F32, kind=dbg_kind)
    a_sp = nc.dram_tensor("a_sp", [D, QL], BF16, kind=dbg_kind)
    xn_dbg = nc.dram_tensor("xn_dbg", [D, S], BF16, kind=dbg_kind) if debug else None
    if debug:
        kT_dbg = nc.dram_tensor("kT_dbg", [128, S], BF16, kind=dbg_kind)
        qT_dbg = nc.dram_tensor("qT_dbg", [128, QL], BF16, kind=dbg_kind)
        vg_dbg = nc.dram_tensor("vg_dbg", [128, NKC * 130], BF16, kind=dbg_kind)
        rr_dbg = nc.dram_tensor("rr_dbg", [128, 512], F32, kind=dbg_kind)
        p16_dbg = nc.dram_tensor("p16_dbg", [128, 1024], BF16, kind=dbg_kind)
        xh_dbg = nc.dram_tensor("xh_dbg", [D, 512], F32, kind=dbg_kind)
        hb_dbg = nc.dram_tensor("hb_dbg", [128, 2048], BF16, kind=dbg_kind)
        mx_dbg = nc.dram_tensor("mx_dbg", [128, 2048], F32, kind=dbg_kind)
        q_dbg = nc.dram_tensor("q_dbg", [D, 512], BF16, kind=dbg_kind)

    xn16 = [nc.alloc_sbuf_tensor(f"xn16_{c}", [128, S], BF16) for c in range(8)]
    kT2 = [nc.alloc_sbuf_tensor(f"kT{i}", [128, S], BF16) for i in range(2)]
    vg = nc.alloc_sbuf_tensor("vg", [128, NKC * 130], BF16)
    qT = nc.alloc_sbuf_tensor("qT", [128, QL], BF16)
    p16 = [[nc.alloc_sbuf_tensor(f"p16_{h}_{b}", [128, 1024], BF16)
            for b in range(2)] for h in range(2)]
    asta1 = [nc.alloc_sbuf_tensor(f"asta_{h}", [64, 512], BF16) for h in range(2)]
    rrowT = nc.alloc_sbuf_tensor("rrowT", [128, 512], F32)
    rrow0 = nc.alloc_sbuf_tensor("rrow0", [1, 512], F32)
    rcb = [nc.alloc_sbuf_tensor(f"rcb{i}", [64, 512], F32) for i in range(2)]
    wst = [nc.alloc_sbuf_tensor(f"wst{i}", [128, 1024], F32) for i in range(2)]
    w16 = {m: nc.alloc_sbuf_tensor(f"w16{m}", [128, 1024], BF16) for m in "qkv"}
    ones16 = nc.alloc_sbuf_tensor("ones16", [128, 128], BF16)
    bocs = nc.alloc_sbuf_tensor("bocs", [128, 8], F32)
    xstage = [nc.alloc_sbuf_tensor(f"xstage{c}", [128, 512], F32) for c in range(8)]

    REGION = 66 * 1024
    arena = nc.alloc_sbuf_tensor("arena", [128, REGION], mybir.dt.uint8)
    arena_addr = nc.lookup_mloc(arena).addr

    def overlay(name, shape, dtype, off):
        per_part = int(np.prod(shape[1:])) * mybir.dt.size(dtype)
        assert off + per_part <= REGION, (name, off, per_part)
        h = nc.virtual_sbuf_tensor(name, shape, dtype)
        nc._tensor(h.name, list(shape), dtype, type="SB")
        ml = nc.lookup_mloc(h)
        ml.allocated = not nc.target_bir_lowering
        ml.addr = arena_addr + off
        return h

    o = 0
    tern16 = []
    for c in range(8):
        tern16.append(overlay(f"tern16_{c}", [128, D], BF16, o)); o += 2048
    xh32 = []
    for j in range(8):
        xh32.append(overlay(f"xh32_{j}", [128, 512], F32, o)); o += 2048
    q16t = []
    for j in range(8):
        q16t.append(overlay(f"q16t_{j}", [128, 512], BF16, 16384 + j * 2048))
    hbufB = []
    for p in range(8):
        hbufB.append(overlay(f"hbufB{p}", [128, 1024], BF16, o)); o += 2048
    O_MX = o
    mx1 = overlay("mx1", [128, 512], F32, o); o += 2048
    mx2 = overlay("mx2", [128, 512], F32, o); o += 2048
    r7B = overlay("r7B", [128, 512], F32, o); o += 2048
    srB = overlay("srB", [128, 512], F32, o); o += 2048
    wmB = overlay("wmB", [128, D], F32, o)
    fi32 = overlay("fi32", [128, 512], I32, o); o += 2048
    ff1 = overlay("ff1", [128, 512], F32, o); o += 2048
    ff2 = overlay("ff2", [128, 512], F32, o); o += 2048
    r_wm = overlay("r_wm", [1, D], F32, 16384 + 4 * 2048)
    r_rw = overlay("r_rw", [1, D], F32, 16384 + 6 * 2048)
    rwB = overlay("rwB", [128, D], F32, O_MX)
    F_END = o
    assert F_END <= REGION

    o = 0
    xb16 = []
    for c in range(8):
        xb16.append(overlay(f"xb16_{c}", [128, 512], BF16, o)); o += 1024
    xsq16 = []
    for c in range(8):
        xsq16.append(overlay(f"xsq16_{c}", [128, 512], BF16, o)); o += 1024
    xodd = []
    for c in range(8):
        xodd.append(overlay(f"xodd_{c}", [128, 512], F32, o)); o += 2048
    r_mean = [overlay("r_mean0", [1, 512], F32, o), overlay("r_mean1", [1, 512], F32, o + 2048)]
    o += 4096
    r_ms = [overlay("r_ms0", [1, 512], F32, o), overlay("r_ms1", [1, 512], F32, o + 2048)]
    o += 4096
    r_c = [overlay("r_c0", [1, 512], F32, o), overlay("r_c1", [1, 512], F32, o + 2048)]
    o += 4096
    meanB = [overlay("meanB0", [128, 512], F32, o), overlay("meanB1", [128, 512], F32, o + 2048)]
    o += 4096
    rsB = [overlay("rsB0", [128, 512], F32, o), overlay("rsB1", [128, 512], F32, o + 2048)]
    o += 4096
    tA = [overlay("tA", [128, 512], F32, o)] * 2
    o += 2048
    assert o <= REGION

    o = 0
    wo16 = []
    for c in range(8):
        wo16.append(overlay(f"wo16_{c}", [128, D], BF16, o)); o += 2048
    arel = []
    for c in range(8):
        arel.append(overlay(f"arel{c}", [128, 512], BF16, o)); o += 1024
    outst = [overlay("outst0", [128, 512], F32, o), overlay("outst1", [128, 512], F32, o + 2048)]
    o += 4096
    D_END = o
    p16x = [overlay(f"p16x{h}", [128, 1024], BF16, D_END + h * 2048) for h in range(2)]
    av65 = [overlay(f"av65_{h}", [65, 512], F32, D_END + 4096 + h * 2048) for h in range(2)]
    qT2b = overlay("qT2b", [128, 2048], BF16, D_END + 8192)
    assert o <= REGION

    fost = [mx1, mx2]

    pbS = [nc.alloc_psum_tensor(f"pbS{i}", [128, 1024], F32) for i in range(2)]
    pbA2 = [[nc.alloc_psum_tensor(f"pbA{h}_{p}", [128, 512], F32) for p in range(2)]
            for h in range(2)]
    pbT = [pbS[0][:, 0:512], pbS[1][:, 0:512]]

    for h in range(2):
        p16[h].append(p16x[h])
    qT2 = [qT, qT2b]
    sems = Sems(nc)

    def walk(em: Em):
        sems.reset()
        st = {"tile": 0, "ev_snap": [],
              "sbank_free": [0, 0],
              "pbuf_free": [[0, 0, 0], [0, 0, 0]],
              "pvpsum_free": [0, 0],
              "w16_free": 0,
              "rcb_free": [0, 0],
              "av_free": [0, 0],
              "rrow_free": [0, 0],
              "asta_free": 0,
              "outst_free": [0, 0],
              }

        def acc_tile(n_mm, mk_mm, evac, banks=None, te_pre=None):
            k = st["tile"]
            st["tile"] += 1
            bank = (banks or pbT)[k % 2]
            if k >= 2:
                em.wait("tensor", "ev", st["ev_snap"][k - 2] + 1)
            if te_pre is not None:
                te_pre()
            for i in range(n_mm):
                em.op("tensor", lambda t, i=i, bank=bank: mk_mm(t, i, bank),
                      sem="ps" if i == n_mm - 1 else None)
            em.wait("vector", "ps", em.cnt("ps"))
            st["ev_snap"].append(em.cnt("ev"))
            evac(bank)

        em.op("gpsimd", lambda g: g.load_library(attn_lib))

        em.op("vector", lambda v: v.memset(ones16[:], 1.0))
        for pb in (pbS[0], pbS[1], pbA2[0][0], pbA2[0][1], pbA2[1][0], pbA2[1][1]):
            em.op("vector", lambda v, pb=pb: v.memset(pb[:], 0.0))
        if debug:
            em.op("vector", lambda v: v.memset(hbufB[0][:], 777.0))
            em.op("vector", lambda v: v.memset(hbufB[1][:], 777.0))
        vgr = vg[:].rearrange("p (t s c) -> p t s c", t=NKC, s=2, c=65)
        em.op("vector", lambda v: v.memset(vgr[:, :, :, 64:65], 1.0))
        em.op("sync", lambda sy: sy.dma_start(out=bocs[:], in_=boc[:]), sem="wd", k=16)

        for tb in range(NB):
            par = tb % 2
            xbl = xstage if par == 0 else xodd
            if tb >= 2:
                em.wait("sync", "xn", tb - 1)
                em.wait("gpsimd", "xn", tb - 1)
            em.wait("sync", "x", em.cnt("x"))
            em.wait("gpsimd", "xg", em.cnt("xg"))
            for c in range(8):
                who = "sync" if c < 4 else "gpsimd"
                em.op(who, lambda e, c=c: e.dma_start(
                    out=xbl[c][:], in_=xT[c * 128:(c + 1) * 128, tb * 512:(tb + 1) * 512]),
                    sem="x" if c < 4 else "xg", k=16)
            em.wait("vector", "x", em.cnt("x"))
            em.wait("vector", "xg", em.cnt("xg"))
            em.wait("scalar", "x", em.cnt("x"))
            em.wait("scalar", "xg", em.cnt("xg"))
            for c in range(8):
                em.op("vector", lambda v, c=c: v.tensor_copy(out=xb16[c][:], in_=xbl[c][:]),
                      sem="xc" if c == 7 else None)
                em.op("scalar", lambda sc, c=c: sc.activation(
                    out=xsq16[c][:], in_=xbl[c][:], func=AF.Square),
                    sem="xq" if c == 7 else None)

            def mk_sum(t, i, bank, src):
                return t.matmul(bank[:], ones16[:], src[i][:],
                                start=(i == 0), stop=(i == 7))

            def ev_sum(bank, dst):
                em.op("vector", lambda v: v.tensor_scalar(
                    out=dst[:], in0=bank[0:1, :],
                    scalar1=1.0 / 1024.0, scalar2=None, op0=OP.mult), sem="ev")

            em.wait("tensor", "xc", em.cnt("xc"))
            em.wait("tensor", "xq", em.cnt("xq"))
            acc_tile(8, lambda t, i, bank: mk_sum(t, i, bank, xb16),
                     lambda bank: ev_sum(bank, r_mean[par]))
            acc_tile(8, lambda t, i, bank: mk_sum(t, i, bank, xsq16),
                     lambda bank: ev_sum(bank, r_ms[par]))
            em.wait("vector", "ev", em.cnt("ev"))
            em.op("vector", lambda v: v.tensor_mul(
                r_c[par][:], r_mean[par][:], r_mean[par][:]))
            em.op("vector", lambda v: v.tensor_sub(
                r_ms[par][:], r_ms[par][:], r_c[par][:]))
            em.op("vector", lambda v: v.tensor_scalar_add(
                r_ms[par][:], r_ms[par][:], MAGIC_EPS), sem="fin")
            em.wait("scalar", "fin", em.cnt("fin"))
            em.op("scalar", lambda sc: sc.activation(
                out=r_c[par][:], in_=r_ms[par][:], func=AF.Sqrt), sem="sd")
            em.wait("vector", "sd", em.cnt("sd"))
            em.op("vector", lambda v: v.reciprocal(
                out=r_c[par][:], in_=r_c[par][:]), sem="rs")
            em.wait("gpsimd", "rs", em.cnt("rs"))
            if tb >= 2:
                em.wait("gpsimd", "xn", tb - 1)
            em.op("gpsimd", lambda g: g.partition_broadcast(
                meanB[par][:], r_mean[par][:], 128))
            em.op("gpsimd", lambda g: g.partition_broadcast(
                rsB[par][:], r_c[par][:], 128), sem="bc")
            em.wait("vector", "bc", em.cnt("bc"))
            for c in range(8):
                em.op("vector", lambda v, c=c: v.tensor_sub(tA[par][:], xbl[c][:], meanB[par][:]))
                em.op("vector", lambda v, c=c: v.tensor_mul(
                    xn16[c][:, tb * 512:(tb + 1) * 512], tA[par][:], rsB[par][:]),
                    sem="xn" if c == 7 else None)

        if debug:
            em.wait("gpsimd", "xn", NB)
            for c in range(8):
                em.op("gpsimd", lambda gp, c=c: gp.dma_start(
                    out=xn_dbg[c * 128:(c + 1) * 128, :], in_=xn16[c][:]), sem="out", k=16)

        for c in range(8):
            em.wait("sync", "wc", em.cnt("wc") - 1)
            em.op("sync", lambda sy, c=c: sy.dma_start(
                out=wst[c % 2][:], in_=wfT[c * 128:(c + 1) * 128, :]), sem="wd", k=16)
            em.wait("scalar", "wd", em.cnt("wd"))
            em.op("scalar", lambda sc, c=c: sc.activation(
                out=tern16[c][:], in_=wst[c % 2][:], func=AF.Abs), sem="wc")

        def mk_wsum(t, i, bank, half):
            return t.matmul(bank[:], ones16[:],
                            tern16[i][:, half * 512:(half + 1) * 512],
                            start=(i == 0), stop=(i == 7))

        for half in range(2):
            def ev_wsum(bank, half=half):
                em.op("vector", lambda v: v.tensor_scalar(
                    out=r_wm[0:1, half * 512:(half + 1) * 512], in0=bank[0:1, :],
                    scalar1=1.0 / 1024.0, scalar2=None, op0=OP.mult), sem="ev")
            em.wait("tensor", "wc", em.cnt("wc"))
            acc_tile(8, lambda t, i, bank, half=half: mk_wsum(t, i, bank, half), ev_wsum)
        em.op("vector", lambda v: v.reciprocal(out=r_rw[:], in_=r_wm[:]), sem="rs")
        em.wait("gpsimd", "rs", em.cnt("rs"))
        em.op("gpsimd", lambda g: g.partition_broadcast(rwB[:], r_rw[:], 128))
        em.op("gpsimd", lambda g: g.partition_broadcast(wmB[:], r_wm[:], 128), sem="bc")

        for c in range(8):
            em.wait("sync", "wc", em.cnt("wc") - 1)
            em.op("sync", lambda sy, c=c: sy.dma_start(
                out=wst[c % 2][:], in_=wfT[c * 128:(c + 1) * 128, :]), sem="wd", k=16)
            em.wait("vector", "wd", em.cnt("wd"))
            em.wait("vector", "bc", em.cnt("bc"))
            em.op("vector", lambda v, c=c: v.tensor_mul(xh32[0][:], wst[c % 2][:, 0:512], rwB[:, 0:512]))
            em.op("vector", lambda v, c=c: v.tensor_mul(xh32[1][:], wst[c % 2][:, 512:1024], rwB[:, 512:1024]))
            for half in range(2):
                em.op("vector", lambda v, c=c, half=half: v.tensor_scalar(
                    out=xh32[2][:], in0=xh32[half][:], scalar1=0.5, scalar2=None, op0=OP.is_gt))
                em.op("vector", lambda v, c=c, half=half: v.scalar_tensor_tensor(
                    out=xh32[2][:], in0=xh32[half][:], scalar=-0.5, in1=xh32[2][:],
                    op0=OP.is_ge, op1=OP.add))
                em.op("vector", lambda v, c=c, half=half: v.tensor_scalar(
                    out=xh32[2][:], in0=xh32[2][:],
                    scalar1=1.0, scalar2=None, op0=OP.subtract))
                em.op("vector", lambda v, c=c, half=half: v.tensor_mul(
                    tern16[c][:, half * 512:(half + 1) * 512], xh32[2][:],
                    wmB[:, half * 512:(half + 1) * 512]),
                    sem="wc" if half == 1 else None)

        em.wait("sync", "xn", NB)
        for j in range(8):
            em.op("sync", lambda sy, j=j: sy.dma_start(
                out=hbufB[j][:], in_=h16d[j * 128:(j + 1) * 128, :]),
                sem="wg", k=16)
        for tc in range(NQC):
            if tc >= 1:
                em.wait("sync", "acst", em.cnt("acst"))
            for c in range(8):
                em.op("sync", lambda sy, c=c, tc=tc: sy.dma_start(
                    out=xstage[c][:], in_=xT[c * 128:(c + 1) * 128, tc * 512:(tc + 1) * 512]),
                    sem="xrel", k=16)
            if tc == 0:
                em.wait("tensor", "wg", em.cnt("wg"))
            for j in range(8):
                def mk_xh(t, i, bank, j=j, tc=tc):
                    return t.matmul(bank[:], hbufB[j][:, i * 128:(i + 1) * 128],
                                    xn16[i][:, tc * 512:(tc + 1) * 512],
                                    start=(i == 0), stop=(i == 7))

                def ev_xh(bank, j=j):
                    em.op("vector", lambda v: v.tensor_copy(out=xh32[j][:], in_=bank[:]))
                    if j == 0:
                        em.wait("vector", "acst", em.cnt("acst"))
                        em.op("vector", lambda v: v.tensor_copy(out=mx1[:], in_=xh32[j][:]))
                        em.op("vector", lambda v: v.tensor_copy(out=mx2[:], in_=xh32[j][:]), sem="ev")
                    else:
                        em.op("vector", lambda v: v.tensor_max(mx1[:], mx1[:], xh32[j][:]))
                        em.op("vector", lambda v: v.tensor_tensor(mx2[:], mx2[:], xh32[j][:], OP.min),
                              sem="ev")
                    if j == 7:
                        em.op("vector", lambda v: v.tensor_scalar_mul(mx2[:], mx2[:], -1.0))
                        em.op("vector", lambda v: v.tensor_max(mx1[:], mx1[:], mx2[:]), sem="mxv")

                acc_tile(8, mk_xh, ev_xh, banks=[pbA2[0][0][:], pbA2[0][1][:]])

            em.wait("gpsimd", "mxv", em.cnt("mxv"))
            if debug and tc == 0:
                em.op("gpsimd", lambda g: g.dma_start(
                    out=mx_dbg[:, 1536:2048], in_=mx1[:]), sem="asp", k=16)
            em.op("gpsimd", lambda g: g.partition_all_reduce(srB[:], mx1[:], 128, ReduceOp.max),
                  sem="mxg")
            em.wait("vector", "mxg", em.cnt("mxg"))
            em.op("vector", lambda v: v.tensor_scalar_max(mx2[:], srB[:], MAGIC_EPS))
            em.op("vector", lambda v: v.reciprocal(out=r7B[:], in_=mx2[:]))
            em.op("vector", lambda v: v.tensor_scalar_mul(r7B[:], r7B[:], 7.0))
            em.op("vector", lambda v: v.tensor_scalar_mul(srB[:], mx2[:], 1.0 / 7.0))
            for j in range(8):
                em.op("vector", lambda v, j=j: v.tensor_mul(ff1[:], xh32[j][:], r7B[:]))
                em.op("vector", lambda v: v.tensor_copy(out=fi32[:], in_=ff1[:]))
                em.op("vector", lambda v: v.tensor_copy(out=ff2[:], in_=fi32[:]))
                em.op("vector", lambda v: v.tensor_sub(ff1[:], ff1[:], ff2[:]))
                em.op("vector", lambda v: v.scalar_tensor_tensor(
                    out=ff2[:], in0=ff1[:], scalar=0.5, in1=ff2[:], op0=OP.is_gt, op1=OP.add))
                em.op("vector", lambda v: v.scalar_tensor_tensor(
                    out=ff2[:], in0=ff1[:], scalar=-0.5, in1=ff2[:], op0=OP.is_ge, op1=OP.add))
                em.op("vector", lambda v: v.tensor_scalar(
                    out=ff2[:], in0=ff2[:], scalar1=1.0, scalar2=7.0, op0=OP.subtract, op1=OP.min))
                em.op("vector", lambda v, j=j: v.tensor_scalar_max(q16t[j][:], ff2[:], -8.0),
                      sem="q16" if j == 7 else None)

            if debug and tc == 0:
                em.wait("gpsimd", "q16", em.cnt("q16"))
                for j in range(8):
                    em.op("gpsimd", lambda gp, j=j: gp.dma_start(
                        out=xh_dbg[j * 128:(j + 1) * 128, :], in_=xh32[j][:]), sem="asp", k=16)
                    em.op("gpsimd", lambda gp, j=j: gp.dma_start(
                        out=q_dbg[j * 128:(j + 1) * 128, :], in_=q16t[j][:]), sem="asp", k=16)
                em.op("gpsimd", lambda gp: gp.dma_start(out=hb_dbg[:, 0:1024], in_=hbufB[0][:]), sem="asp", k=16)
                em.op("gpsimd", lambda gp: gp.dma_start(out=hb_dbg[:, 1024:2048], in_=hbufB[1][:]), sem="asp", k=16)
                em.op("gpsimd", lambda gp: gp.dma_start(out=mx_dbg[:, 0:512], in_=mx1[:]), sem="asp", k=16)
                em.op("gpsimd", lambda gp: gp.dma_start(out=mx_dbg[:, 512:1024], in_=r7B[:]), sem="asp", k=16)
                em.op("gpsimd", lambda gp: gp.dma_start(out=mx_dbg[:, 1024:1536], in_=srB[:]), sem="asp", k=16)
                em.wait("tensor", "asp", em.cnt("asp"))

            for o in range(8):
                def mk_ffn(t, i, bank, o=o):
                    return t.matmul(bank[:], tern16[i][:, o * 128:(o + 1) * 128],
                                    q16t[i][:], start=(i == 0), stop=(i == 7))

                def ev_ffn(bank, o=o, tc=tc):
                    par = o % 2
                    em.wait("vector", "acst", st["outst_free"][par])
                    em.op("vector", lambda v: v.tensor_mul(ff1[:], bank[:], srB[:]))
                    em.wait("vector", "xrel", em.cnt("xrel"))
                    em.op("vector", lambda v: v.tensor_scalar_add(ff2[:], xstage[o][:], bocs[:, o:o + 1]))
                    em.op("vector", lambda v: v.tensor_add(fost[par][:], ff1[:], ff2[:]), sem="ev")
                    em.wait("gpsimd", "ev", em.cnt("ev"))
                    em.op("gpsimd", lambda g: g.dma_start(
                        out=acc_sp[o * 128:(o + 1) * 128, tc * 512:(tc + 1) * 512],
                        in_=fost[par][:]), sem="acst", k=16)
                    st["outst_free"][par] = em.cnt("acst")

                em.wait("tensor", "q16", em.cnt("q16"))
                em.wait("tensor", "wc", em.cnt("wc"))
                acc_tile(8, mk_ffn, ev_ffn, banks=[pbA2[1][0][:], pbA2[1][1][:]])

        def load_w_group(g):
            for mi, (mat, wdr) in enumerate((("q", wq), ("k", wk), ("v", wv))):
                em.wait("sync", "wc", em.cnt("wc"))
                em.op("sync", lambda sy, wdr=wdr, g=g, mi=mi: sy.dma_start(
                    out=wst[mi % 2][:], in_=wdr[g * 128:(g + 1) * 128, :]),
                    sem="wd", k=16)
                em.wait("vector", "wd", em.cnt("wd"))
                em.wait("vector", "ps", st["w16_free"])
                em.op("vector", lambda v, mat=mat, mi=mi: v.tensor_copy(
                    out=w16[mat][:], in_=wst[mi % 2][:]), sem="wc")

        def emit_k_tiles(g, banks, te_pre_first):
            for t8 in range(8):
                def mk_k(t, i, bank, t8=t8, g=g):
                    return t.matmul(bank[:], w16["k"][:, i * 128:(i + 1) * 128],
                                    xn16[i][:, t8 * 512:(t8 + 1) * 512],
                                    start=(i == 0), stop=(i == 7))

                def ev_k(bank, t8=t8, g=g):
                    em.op("vector", lambda v: v.tensor_copy(
                        out=kT2[g % 2][:, t8 * 512:(t8 + 1) * 512], in_=bank[:]), sem="ev")
                acc_tile(8, mk_k, ev_k, banks=banks,
                         te_pre=te_pre_first if t8 == 0 else None)

        load_w_group(0)
        for g in range(NG):
            em.wait("tensor", "xn", NB)
            em.wait("tensor", "wc", em.cnt("wc"))
            em.wait("tensor", "exp", em.cnt("exp"))
            if g == 0:
                for tq in range(NQC):
                    def mk_q(t, i, bank, tq=tq):
                        return t.matmul(bank[:], w16["q"][:, i * 128:(i + 1) * 128],
                                        xn16[i][:, tq * 512:(tq + 1) * 512],
                                        start=(i == 0), stop=(i == 7))

                    def ev_q(bank, tq=tq):
                        em.op("vector", lambda v: v.tensor_scalar(
                            out=qT2[0][:, tq * 512:(tq + 1) * 512], in0=bank[:],
                            scalar1=0.125, scalar2=None, op0=OP.mult), sem="ev")
                    acc_tile(8, mk_q, ev_q)
            if g == 0:
                emit_k_tiles(0, pbT, None)
            def emit_v_tile(tt):
                def mk_v(t, i, bank, tt=tt):
                    return t.matmul(bank[:, 0:128], xn16[i][:, tt * 128:(tt + 1) * 128],
                                    w16["v"][:, i * 128:(i + 1) * 128],
                                    start=(i == 0), stop=(i == 7))

                def ev_v(bank, tt=tt):
                    em.op("vector", lambda v: v.tensor_copy(
                        out=vgr[:, tt, :, 0:64],
                        in_=bank[:, 0:128].rearrange("p (s c) -> p s c", s=2)), sem="ev")

                def pre_v(tt=tt):
                    if tt < 2:
                        em.wait("tensor", "a16", st["pvpsum_free"][1])
                acc_tile(8, mk_v, ev_v, banks=[pbA2[0][1][:], pbA2[1][1][:]], te_pre=pre_v)
                st["v_ready"][tt] = em.cnt("ev")

            st["v_ready"] = [0] * NKC
            st.setdefault("k_ev_snap", 0)

            for qc in range(NQC):
                apar = qc % 2
                exp_done = [[0] * NKG for _ in range(2)]
                for kg in range(NKG + 2):
                    if qc == 0 and kg == 0:
                        st["qk_ev_snap"] = em.cnt("ev")
                    if kg < NKG:
                        for hh in range(2):
                            em.wait("tensor", "exp", st["sbank_free"][hh])
                            if qc == 0 and kg == 0 and hh == 0:
                                em.wait("tensor", "ev", st["qk_ev_snap"])
                            for half in range(2):
                                kc = 2 * kg + half
                                em.op("tensor", lambda t, hh=hh, half=half, kc=kc, qc=qc: t.matmul(
                                    pbS[hh][:, half * 512:(half + 1) * 512],
                                    kT2[g % 2][hh * 64:(hh + 1) * 64, kc * 128:(kc + 1) * 128],
                                    qT2[g % 2][hh * 64:(hh + 1) * 64, qc * 512:(qc + 1) * 512],
                                    start=True, stop=True),
                                    sem="qk" if half == 1 else None)
                            em.wait("scalar", "qk", em.cnt("qk"))
                            em.wait("scalar", "pvp", st["pbuf_free"][hh][kg % 3])
                            em.op("scalar", lambda sc, hh=hh, kg=kg: sc.activation(
                                out=p16[hh][kg % 3][:], in_=pbS[hh][:], func=AF.Exp), sem="exp")
                            st["sbank_free"][hh] = em.cnt("exp")
                            exp_done[hh][kg] = em.cnt("exp")
                    if kg >= 2:
                        pk = kg - 2
                        for hh in range(2):
                            em.wait("tensor", "exp", exp_done[hh][pk])
                            if qc == 0 and hh == 0:
                                em.wait("tensor", "ev", st["v_ready"][2 * pk + 1])
                            if pk == 0:
                                em.wait("tensor", "a16", st["pvpsum_free"][apar])
                                if qc == 1 and hh == 0:
                                    em.wait("tensor", "ev", st["v_ready"][NKC - 1])
                                if qc == 3 and hh == 0 and st["k_ev_snap"] > 0:
                                    em.wait("tensor", "ev", st["k_ev_snap"])
                            for half in range(2):
                                kc = 2 * pk + half
                                em.op("tensor", lambda t, hh=hh, half=half, kc=kc, pk=pk, apar=apar: t.matmul(
                                    pbA2[hh][apar][0:65, :],
                                    vgr[:, kc, hh, :],
                                    p16[hh][pk % 3][:, half * 512:(half + 1) * 512],
                                    start=(kc == 0), stop=(kc == NKC - 1)),
                                    sem="pvp" if half == 1 else None)
                            st["pbuf_free"][hh][pk % 3] = em.cnt("pvp")
                    if qc == 2 and kg % 2 == 0 and kg < NKG and g + 1 < NG:
                        t8 = kg // 2

                        def mk_k2(t, i, bank, t8=t8, gg=g + 1):
                            return t.matmul(bank[:], w16["k"][:, i * 128:(i + 1) * 128],
                                            xn16[i][:, t8 * 512:(t8 + 1) * 512],
                                            start=(i == 0), stop=(i == 7))

                        def ev_k2(bank, t8=t8, gg=g + 1):
                            em.op("vector", lambda v: v.tensor_copy(
                                out=kT2[gg % 2][:, t8 * 512:(t8 + 1) * 512],
                                in_=bank[:]), sem="ev")

                        def pre_k2(t8=t8):
                            if t8 == 0:
                                em.wait("tensor", "a16", st["pvpsum_free"][1])
                        acc_tile(8, mk_k2, ev_k2,
                                 banks=[pbA2[0][1][:], pbA2[1][1][:]], te_pre=pre_k2)
                        st["k_ev_snap"] = em.cnt("ev")
                    if qc == 3 and kg % 2 == 1 and kg < 8 and g + 1 < NG:
                        tq = kg // 2

                        def mk_q2(t, i, bank, tq=tq):
                            return t.matmul(bank[:], w16["q"][:, i * 128:(i + 1) * 128],
                                            xn16[i][:, tq * 512:(tq + 1) * 512],
                                            start=(i == 0), stop=(i == 7))

                        def ev_q2(bank, tq=tq, gg=g + 1):
                            em.op("vector", lambda v: v.tensor_scalar(
                                out=qT2[gg % 2][:, tq * 512:(tq + 1) * 512], in0=bank[:],
                                scalar1=0.125, scalar2=None, op0=OP.mult), sem="ev")

                        def pre_q2(tq=tq):
                            if tq == 0:
                                em.wait("tensor", "a16", st["pvpsum_free"][0])
                                em.wait("tensor", "wc", em.cnt("wc"))
                        acc_tile(8, mk_q2, ev_q2,
                                 banks=[pbA2[0][0][:], pbA2[1][0][:]], te_pre=pre_q2)
                    if qc == 0 and kg < NKG:
                        for tt in (2 * kg, 2 * kg + 1):
                            emit_v_tile(tt)
                        if kg == NKG - 1:
                            st["w16_free"] = em.cnt("ps")
                            if g + 1 < NG:
                                load_w_group(g + 1)

                em.wait("vector", "pvp", em.cnt("pvp"))
                for hh in range(2):
                    em.wait("vector", "mlt", st["av_free"][hh])
                    em.op("vector", lambda v, hh=hh, apar=apar: v.tensor_copy(
                        out=av65[hh][:], in_=pbA2[hh][apar][0:65, :]), sem="a16")
                st["pvpsum_free"][apar] = em.cnt("a16")
                st["av_free"] = [em.cnt("mlt") + 1, em.cnt("mlt") + 2]
                for hh in range(2):
                    em.wait("vector", "rcb", st["rrow_free"][hh])
                    em.op("vector", lambda v, hh=hh: v.reciprocal(
                        out=av65[hh][64:65, :], in_=av65[hh][64:65, :]))
                    em.op("vector", lambda v, hh=hh: v.tensor_copy(
                        out=rrow0[:], in_=av65[hh][64:65, :]), sem="rcp")
                    em.wait("gpsimd", "rcp", em.cnt("rcp"))
                    em.wait("gpsimd", "mlt", st["rcb_free"][hh])
                    em.op("gpsimd", lambda gp, hh=hh: gp.partition_broadcast(
                        rcb[hh][:], rrow0[:], 64), sem="rcb")
                    st["rrow_free"][hh] = em.cnt("rcb")
                    st["rcb_free"][hh] = em.cnt("mlt") + 1 + hh
                em.wait("vector", "rcb", em.cnt("rcb"))
                em.wait("vector", "asp", st["asta_free"])
                for hh in range(2):
                    em.op("vector", lambda v, hh=hh: v.tensor_mul(
                        asta1[hh][:], av65[hh][0:64, :], rcb[hh][:]), sem="mlt")
                em.wait("gpsimd", "mlt", em.cnt("mlt"))
                for hh in range(2):
                    em.op("gpsimd", lambda gp, g=g, qc=qc, hh=hh: gp.dma_start(
                        out=a_sp[g * 128 + hh * 64:g * 128 + (hh + 1) * 64,
                                 qc * 512:(qc + 1) * 512],
                        in_=asta1[hh][:]), sem="asp", k=16)
                st["asta_free"] = em.cnt("asp")
                if g == NG - 1 and qc == 0:
                    for c in range(8):
                        em.wait("sync", "wc", em.cnt("wc") - (0 if c < 2 else 1))
                        em.op("sync", lambda sy, c=c: sy.dma_start(
                            out=wst[c % 2][:], in_=wo[c * 128:(c + 1) * 128, :]),
                            sem="wd", k=16)
                        em.wait("vector", "wd", em.cnt("wd"))
                        em.op("vector", lambda v, c=c: v.tensor_copy(
                            out=wo16[c][:], in_=wst[c % 2][:]), sem="wc")

        st["outst_free"] = [0, 0]

        for tc in range(NQC):
            em.wait("sync", "asp", em.cnt("asp"))
            if tc >= 1:
                em.wait("sync", "ps", em.cnt("ps"))
                em.wait("sync", "ev", em.cnt("ev"))
            for hd in range(8):
                em.op("sync", lambda sy, hd=hd, tc=tc: sy.dma_start(
                    out=arel[hd][:], in_=a_sp[hd * 128:(hd + 1) * 128, tc * 512:(tc + 1) * 512]),
                    sem="arel", k=16)
            em.wait("sync", "acst", em.cnt("acst"))
            for o in range(8):
                em.op("sync", lambda sy, o=o, tc=tc: sy.dma_start(
                    out=xstage[o][:], in_=acc_sp[o * 128:(o + 1) * 128, tc * 512:(tc + 1) * 512]),
                    sem="accr", k=16)
            for o in range(8):
                def mk_wo(t, i, bank, o=o):
                    return t.matmul(bank[:], wo16[i][:, o * 128:(o + 1) * 128],
                                    arel[i][:], start=(i == 0), stop=(i == 7))

                def ev_wo(bank, o=o, tc=tc):
                    par = o % 2
                    em.wait("vector", "out", st["outst_free"][par])
                    em.wait("vector", "accr", em.cnt("accr"))
                    em.op("vector", lambda v: v.tensor_add(outst[par][:], xstage[o][:], bank[:]),
                          sem="ev")
                    em.wait("gpsimd", "ev", em.cnt("ev"))
                    em.op("gpsimd", lambda gp: gp.dma_start(
                        out=outT[o * 128:(o + 1) * 128, tc * 512:(tc + 1) * 512],
                        in_=outst[par][:]), sem="out", k=16)
                    st["outst_free"][par] = em.cnt("out")

                em.wait("tensor", "arel", em.cnt("arel"))
                acc_tile(8, mk_wo, ev_wo)

        em.wait("gpsimd", "out", em.cnt("out"))
        em.wait("sync", "out", em.cnt("out"))
        if debug:
            em.op("sync", lambda sy: sy.dma_start(out=kT_dbg[:], in_=kT2[(NG - 1) % 2][:]), sem="arel", k=16)
            em.op("sync", lambda sy: sy.dma_start(out=qT_dbg[:], in_=qT[:]), sem="arel", k=16)
            em.op("sync", lambda sy: sy.dma_start(out=vg_dbg[:], in_=vg[:]), sem="arel", k=16)
            em.op("sync", lambda sy: sy.dma_start(out=rr_dbg[:], in_=rrowT[:]), sem="arel", k=16)
            em.op("sync", lambda sy: sy.dma_start(out=p16_dbg[:], in_=p16[0][1][:]), sem="arel", k=16)
            em.wait("sync", "arel", em.cnt("arel"))


    with nc.Block() as blk:
        final_counts = {}

        def run(engine_name):
            def f(eng):
                em = Em(engine_name, eng, sems)
                walk(em)
                if final_counts:
                    assert final_counts == sems.n, (
                        "sem count divergence across walks", engine_name,
                        {k: (final_counts[k], sems.n[k]) for k in sems.n
                         if final_counts[k] != sems.n[k]})
                else:
                    final_counts.update(sems.n)
            return f

        blk.sync(run("sync"))
        blk.vector(run("vector"))
        blk.scalar(run("scalar"))
        blk.tensor(run("tensor"))
        blk.gpsimd(run("gpsimd"))

    lower_extended_insts(nc)
    return nc


_NC_CACHE = None


def _get_nc():
    global _NC_CACHE
    if _NC_CACHE is None:
        _NC_CACHE = build_core_program()
    return _NC_CACHE


def _tile_cols(w):
    return np.ascontiguousarray(
        w.reshape(8, 128, 8, 128).transpose(2, 1, 0, 3).reshape(1024, 1024))


def _hadamard_bf16():
    h = np.array([[1.0]], dtype=np.float64)
    while h.shape[0] < D:
        h = np.block([[h, h], [h, -h]])
    return (h / np.sqrt(D)).astype(np.float32).astype(ml_dtypes.bfloat16)


def kernel(L0_attn_x, L0_attn_Wq, L0_attn_Wk, L0_attn_Wv, L0_attn_Wo,
           L0_attn_bo, L0_ffn_weight):
    x = np.asarray(L0_attn_x, dtype=np.float32)
    wq = _tile_cols(np.asarray(L0_attn_Wq, dtype=np.float32))
    wk = _tile_cols(np.asarray(L0_attn_Wk, dtype=np.float32))
    wv = _tile_cols(np.asarray(L0_attn_Wv, dtype=np.float32))
    wo = np.ascontiguousarray(np.asarray(L0_attn_Wo, dtype=np.float32))
    bo = np.asarray(L0_attn_bo, dtype=np.float32)
    wf = np.asarray(L0_ffn_weight, dtype=np.float32)

    boc = np.ascontiguousarray(bo.reshape(8, 128).T)
    wfT = np.ascontiguousarray(wf.T)
    h16 = np.ascontiguousarray(
        _hadamard_bf16().reshape(8, 128, 8, 128).transpose(2, 1, 0, 3)
        .reshape(1024, 1024))

    nc = _get_nc()
    in_maps = []
    for core in range(8):
        b, qh = core // 2, core % 2
        xt = np.ascontiguousarray(np.roll(x[b].T, -qh * QL, axis=1))
        in_maps.append({
            "xT": xt, "wq": wq, "wk": wk, "wv": wv, "wo": wo,
            "boc": boc, "wfT": wfT, "h16d": h16,
        })
    res = run_bass_kernel_spmd(nc, in_maps, core_ids=list(range(8)))
    out = np.empty((4, S, D), dtype=np.float32)
    for core in range(8):
        b, qh = core // 2, core % 2
        out[b, qh * QL:(qh + 1) * QL, :] = res.results[core]["outT"].T
    return out
